# version 06
# baseline (speedup 1.0000x reference)
import numpy as np
import ml_dtypes
import concourse.bacc as bacc
import concourse.mybir as mybir
from concourse.bass_isa import ReduceOp
from concourse.bass_utils import run_bass_kernel_spmd
from contextlib import ExitStack

F32 = mybir.dt.float32
BF16 = mybir.dt.bfloat16
I8 = mybir.dt.int8
OP = mybir.AluOpType
ACT = mybir.ActivationFunctionType
AX = mybir.AxisListType.X

B, S, D, DI = 2, 4096, 2048, 8192
CH = 1024
NSTG = 4
KC = D // 128
OC = DI // 128

ENGS = ("sync", "tensor", "vector", "scalar", "gpsimd")
_CACHED = {}
STOP = "full"
DBG_SCRATCH = False
DBG_BARRIER = False
DBG_SKIPA2 = False
DBG_A2 = ""
DBG_SERDGE = False


class Prog:
    def __init__(self):
        self.ops = {e: [] for e in ENGS}
        self.cnt = {}

    def inc(self, sem, amt):
        self.cnt[sem] = self.cnt.get(sem, 0) + amt
        return self.cnt[sem]

    def cur(self, sem):
        return self.cnt.get(sem, 0)

    def emit(self, eng, fn):
        self.ops[eng].append(fn)

    def wait(self, eng, semh, val):
        if val and val > 0:
            self.emit(eng, lambda e, s=semh, v=val: e.wait_ge(s, v))

    def dma(self, eng, out, in_, semh, semname):
        v = self.inc(semname, 16)
        self.emit(eng, lambda e, o=out, i=in_, s=semh: e.dma_start(out=o, in_=i).then_inc(s, 16))
        return v

    def dmat(self, eng, out, in_, semh, semname):
        v = self.inc(semname, 16)
        self.emit(eng, lambda e, o=out, i=in_, s=semh: e.dma_start(out=o, in_=i, transpose=True).then_inc(s, 16))
        return v

    def op(self, eng, fn, sem=None, semh=None):
        if sem is None:
            self.emit(eng, fn)
            if eng in ("vector", "scalar", "gpsimd"):
                self.emit(eng, lambda e: e.drain())
            return None
        v = self.inc(sem, 1)
        self.emit(eng, lambda e, f=fn, s=semh: f(e).then_inc(s, 1))
        if eng in ("vector", "scalar", "gpsimd"):
            self.emit(eng, lambda e: e.drain())
        return v


def build_nc():
    nc = bacc.Bacc("TRN2", target_bir_lowering=False, dynamic_dma_scratch_size=8192,
                   detect_race_conditions=False)

    dram = {}
    dram["xp"] = nc.dram_tensor("xp", [S, D], F32, kind="ExternalInput").ap()
    dram["xo"] = nc.dram_tensor("xo", [CH, D], F32, kind="ExternalInput").ap()
    dram["sel"] = nc.dram_tensor("sel", [128, NSTG], F32, kind="ExternalInput").ap()
    for nm, shp in (("wr", [D, D]), ("wf", [D, D]), ("wc", [D, D]), ("wo", [D, D]),
                    ("wg", [DI, D]), ("wu", [DI, D]), ("wd", [D, DI])):
        dram[nm] = nc.dram_tensor(nm, shp, F32, kind="ExternalInput").ap()
    dram["gam"] = nc.dram_tensor("gam", [128, KC], F32, kind="ExternalInput").ap()
    dram["gsub"] = nc.dram_tensor("gsub", [128, OC], F32, kind="ExternalInput").ap()
    dram["idb"] = nc.dram_tensor("idb", [128, 128], BF16, kind="ExternalInput").ap()
    dram["out"] = nc.dram_tensor("out", [CH, D], F32, kind="ExternalOutput").ap()
    scr_kind = "ExternalOutput" if DBG_SCRATCH else "Internal"
    if DBG_SCRATCH:
        dram["xn_dbg"] = nc.dram_tensor("xn_dbg", [NSTG * 128, KC * CH], BF16,
                                        kind="ExternalOutput").ap()
    dram["h_scr"] = nc.dram_tensor("h_scr", [D, CH], BF16, kind=scr_kind).ap()
    dram["fc_scr"] = nc.dram_tensor("fc_scr", [2 * D, S], BF16, kind=scr_kind).ap()
    dram["act_scr"] = nc.dram_tensor("act_scr", [DI, CH], F32, kind=scr_kind).ap()
    dram["s_scr"] = nc.dram_tensor("s_scr", [D, CH], BF16, kind=scr_kind).ap()
    dram["td_scr"] = nc.dram_tensor("td_scr", [OC * 128, D], BF16, kind=scr_kind).ap()

    with ExitStack() as stk:
        def sb(name, shape, dtype=F32):
            return stk.enter_context(nc.sbuf_tensor(name, shape, dtype))

        t = {}
        t["gam"] = sb("gam_s", [128, KC])
        t["gsub"] = sb("gsub_s", [128, OC])
        t["idb"] = sb("idb_s", [128, 128], BF16)
        t["onesb"] = sb("onesb", [128, 1], BF16)
        t["cols"] = sb("cols", [128, 32])
        t["clipc"] = sb("clipc", [128, 8])
        t["clipb"] = sb("clipb", [128, 16], BF16)
        t["sel"] = sb("sel_s", [128, NSTG])
        t["acc3"] = sb("acc3", [128, 4])
        t["mcols"] = sb("mcols", [128, 8])
        t["mvals"] = sb("mvals", [1, 8])
        t["rows"] = sb("rows", [1, 4 * CH])
        trf = sb("TRF", [128, S], F32)
        trc = sb("TRC", [128, S], F32)
        t["TRF"], t["TRC"] = trf, trc
        t["P"] = [trf[:, 0:D], trf[:, D:2 * D], trc[:, 0:D], trc[:, D:2 * D]]
        t["Q"] = [sb(f"Q{i}", [128, CH]) for i in range(6)]
        t["T1"] = sb("T1", [128, CH])
        t["T2"] = sb("T2", [128, CH])
        t["W"] = [sb(f"W{i}", [128, CH]) for i in range(2)]
        t["U"] = sb("U", [128, CH])
        t["big"] = sb("big", [128, 2 * KC * CH], BF16)
        t["qf"] = sb("qf", [128, KC * CH], BF16)
        strip4 = sb("strip4", [128, 4 * CH], BF16)
        t["strips"] = [strip4[:, i * CH:(i + 1) * CH] for i in range(4)]
        t["tT"] = [strip4[:, 0:D], strip4[:, D:2 * D]]
        t["tdT"] = sb("tdT", [128, D], BF16)
        t["xnb"] = sb("xnb", [128, D], BF16)
        t["tnat"] = t["xnb"]
        t["sqb"] = sb("sqb", [128, CH], BF16)
        t["hstg"] = sb("hstg", [128, CH], BF16)
        hrd1 = sb("hrd1", [128, CH], BF16)
        t["hrd"] = [t["hstg"], hrd1]
        t["sbf"] = [hrd1[:, 0:512], hrd1[:, 512:CH]]
        t["i8"] = sb("i8t", [128, D], I8)
        for nm in ("psA", "psB", "psC", "psD", "psE", "psF"):
            t[nm] = stk.enter_context(nc.psum_tensor(nm, [128, 512], F32))
        t["psBF"] = stk.enter_context(nc.psum_tensor("psBF", [128, 512], F32))
        t["psG"] = stk.enter_context(nc.psum_tensor("psG", [128, 512], F32))

        sems = {}
        for name in ("ld", "gld", "sld", "T", "V", "Sg", "G"):
            sems[name] = stk.enter_context(nc.semaphore(name))
        block = stk.enter_context(nc.Block())

        pr = Prog()
        _stop.ctx = (t, dram) if DBG_SCRATCH else None
        plan(pr, sems, t, dram)

        @block.sync
        def _(e):
            for f in pr.ops["sync"]:
                f(e)

        @block.tensor
        def _(e):
            for f in pr.ops["tensor"]:
                f(e)

        @block.vector
        def _(e):
            for f in pr.ops["vector"]:
                f(e)

        @block.scalar
        def _(e):
            for f in pr.ops["scalar"]:
                f(e)

        @block.gpsimd
        def _(e):
            for f in pr.ops["gpsimd"]:
                f(e)

    nc.finalize()
    return nc


def _stop(pr, sems, tag):
    if STOP != tag:
        return False
    ctx = getattr(_stop, "ctx", None)
    if ctx is not None:
        t, dram = ctx
        if "xn_dbg" in dram:
            pr.wait("sync", sems["V"], pr.cur("V"))
            pr.wait("sync", sems["Sg"], pr.cur("Sg"))
            pr.wait("sync", sems["T"], pr.cur("T"))
            pr.dma("sync", dram["xn_dbg"][0:128, :], t["big"][:, 0:KC * CH],
                   sems["ld"], "ld")
    pr.wait("gpsimd", sems["ld"], pr.cur("ld"))
    pr.wait("gpsimd", sems["gld"], pr.cur("gld"))
    pr.wait("gpsimd", sems["sld"], pr.cur("sld"))
    return True


def plan(pr, sems, t, dram):
    ld, gld, sld = sems["ld"], sems["gld"], sems["sld"]
    T, V, Sg, G = sems["T"], sems["V"], sems["Sg"], sems["G"]
    P, Q, W = t["P"], t["Q"], t["W"]
    big, qf, strips, tT = t["big"], t["qf"], t["strips"], t["tT"]
    cols, clipc, clipb = t["cols"], t["clipc"], t["clipb"]
    acc3, mcols, mvals, rows = t["acc3"], t["mcols"], t["mvals"], t["rows"]
    gam, gsub, idb, onesb = t["gam"], t["gsub"], t["idb"], t["onesb"]
    xnb, sqb, hstg, hrd, sbf, i8 = t["xnb"], t["sqb"], t["hstg"], t["hrd"], t["sbf"], t["i8"]
    tnat, U, T1, T2, tdT = t["tnat"], t["U"], t["T1"], t["T2"], t["tdT"]
    psA, psB, psC, psD = t["psA"], t["psB"], t["psC"], t["psD"]
    psE, psF, psBF = t["psE"], t["psF"], t["psBF"]
    psG = t["psG"]
    xp, out_d = dram["xp"], dram["out"]
    fc_scr = dram["fc_scr"]
    h_scr, act_scr, s_scr, td_scr = dram["h_scr"], dram["act_scr"], dram["s_scr"], dram["td_scr"]

    xn_feat = big[:, 0:KC * CH]
    rh = big[:, KC * CH:2 * KC * CH]
    L = [rows[0:1, i * CH:(i + 1) * CH] for i in range(4)]

    st = {}

    ctok = 0
    for dst, src in ((gam[:], dram["gam"]), (gsub[:], dram["gsub"]), (idb[:], dram["idb"]),
                     (t["sel"][:], dram["sel"])):
        ctok = pr.dma("sync", dst, src, ld, "ld")
    pr.wait("vector", ld, ctok)
    pr.op("vector", lambda e: e.memset(acc3[:], 0.0))
    pr.op("vector", lambda e: e.memset(onesb[:], 1.0), "V", V)
    if _stop(pr, sems, "c0"):
        return

    pre_tiles = []
    for mi, wa in enumerate((dram["wg"], dram["wu"], dram["wd"])):
        rb, hb = wa.shape[0] // 128, wa.shape[1] // CH
        for r in range(rb):
            for h in range(hb):
                pre_tiles.append((mi, wa[r * 128:(r + 1) * 128, h * CH:(h + 1) * CH]))
    n_pre = len(pre_tiles)
    pre = {"i": 0}

    def finalize_matrix(mi):
        pr.op("vector", lambda e, m=mi: e.tensor_copy(clipb[:, 0:1], acc3[:, m:m + 1]))
        pr.op("vector", lambda e, m=mi: e.scalar_tensor_tensor(
            cols[:, 30:31], clipb[:, 0:1], -1.0, acc3[:, m:m + 1], OP.mult, OP.add))
        vv = pr.op("vector", lambda e: e.tensor_copy(clipb[:, 1:2], cols[:, 30:31]), "V", V)
        pr.wait("tensor", V, vv)
        pr.op("tensor", lambda e: e.matmul(
            psE[0:1, 0:1], clipb[:, 0:1], onesb[:, 0:1], start=True, stop=False))
        tv = pr.op("tensor", lambda e: e.matmul(
            psE[0:1, 0:1], clipb[:, 1:2], onesb[:, 0:1], start=False, stop=True), "T", T)
        pr.wait("vector", T, tv)
        vv = pr.op("vector", lambda e, m=mi: e.tensor_scalar(
            mvals[0:1, m:m + 1], psE[0:1, 0:1], 1.0 / (DI * D), 1e-5, OP.mult, OP.max), "V", V)
        pr.wait("gpsimd", V, vv)
        gv = pr.op("gpsimd", lambda e, m=mi: e.partition_broadcast(
            cols[:, 31:32], mvals[0:1, m:m + 1], channels=128), "G", G)
        pr.wait("vector", G, gv)
        pr.op("vector", lambda e, m=mi: e.tensor_scalar(
            mcols[:, 2 * m:2 * m + 1], cols[:, 31:32], 0.5, None, OP.mult))
        pr.op("vector", lambda e, m=mi: e.tensor_scalar(
            mcols[:, 2 * m + 1:2 * m + 2], cols[:, 31:32], -0.5, None, OP.mult), "V", V)

    def pump_prepass(n):
        for _ in range(n):
            i = pre["i"]
            if i >= n_pre:
                return
            mi, ap = pre_tiles[i]
            buf = Q[4 + (i % 2)]
            pr.wait("sync", V, pre.get(("v", i - 2), 0))
            tok = pr.dma("sync", buf[:], ap, ld, "ld")
            pr.wait("scalar", ld, tok)
            s_abs = pr.op("scalar", lambda e, b=buf: e.activation(
                U[:, 0:CH], b[:], ACT.Abs, accum_out=cols[:, 29:30]), "Sg", Sg)
            pr.wait("vector", Sg, s_abs)
            pre[("v", i)] = pr.op("vector", lambda e, m=mi: e.tensor_add(
                acc3[:, m:m + 1], acc3[:, m:m + 1], cols[:, 29:30]), "V", V)
            pre["i"] = i + 1

    stg_tok = {}

    def load_strip(w_ap, c, b0, t_guard, sq):
        tok_out = None
        for hk in range(2):
            stg = Q[sq + hk]
            src = w_ap[hk * 1024:(hk + 1) * 1024, c * 128:(c + 1) * 128].rearrange(
                "(k p) m -> p k m", p=128)
            dst3 = stg[:].rearrange("p (k m) -> p k m", k=8)
            pr.wait("sync", Sg, stg_tok.get(sq + hk, 0))
            tk = pr.dma("sync", dst3, src, ld, "ld")
            pr.wait("scalar", ld, tk)
            if hk == 0 and t_guard:
                pr.wait("scalar", T, t_guard)
            tok_out = pr.op("scalar", lambda e, d=strips[b0 + hk], s2=stg: e.copy(d[:], s2[:]), "Sg", Sg)
            stg_tok[sq + hk] = tok_out
        return tok_out

    def mm_block(banks, lhs_fn, rhs_fn, n=512, sem=None, semh=None):
        tok = None
        for half in range(2):
            bank = banks[half]
            for k in range(KC):
                lhs = lhs_fn(k)
                rhs = rhs_fn(k, half)
                fn = lambda e, bk=bank, l=lhs, r=rhs, kk=k, nn=n: e.matmul(
                    bk[:, 0:nn], l, r, start=(kk == 0), stop=(kk == KC - 1))
                if sem is not None and half == 1 and k == KC - 1:
                    tok = pr.op("tensor", fn, sem, semh)
                else:
                    pr.op("tensor", fn)
        return tok

    def ln_tile(x_src, i, tok_key, quant):
        xb = P[i % 2]
        tok = st[(tok_key, i)]
        pr.wait("scalar", ld, tok)
        pr.wait("scalar", V, st.get("p3_v", 0))
        s_sq = pr.op("scalar", lambda e, b=xb: e.activation(
            P[3][:], b[:], ACT.Square, accum_out=cols[:, 0:1]), "Sg", Sg)
        if _stop(pr, sems, "li1"):
            return True
        pr.wait("vector", ld, tok)
        pr.op("vector", lambda e, b=xb: e.reduce_sum(cols[:, 1:2], b[:], axis=AX))
        pr.wait("vector", Sg, s_sq)
        pr.op("vector", lambda e: e.tensor_scalar(cols[:, 2:3], cols[:, 1:2], 1.0 / D, None, OP.mult))
        pr.op("vector", lambda e: e.tensor_scalar(cols[:, 3:4], cols[:, 0:1], 1.0 / D, None, OP.mult))
        pr.op("vector", lambda e: e.tensor_tensor(cols[:, 4:5], cols[:, 2:3], cols[:, 2:3], OP.mult))
        pr.op("vector", lambda e: e.tensor_tensor(cols[:, 5:6], cols[:, 3:4], cols[:, 4:5], OP.subtract))
        v_ve = pr.op("vector", lambda e: e.tensor_scalar(cols[:, 6:7], cols[:, 5:6], 1e-5, None, OP.add), "V", V)
        if _stop(pr, sems, "li2"):
            return True
        pr.wait("scalar", V, v_ve)
        s_sr = pr.op("scalar", lambda e: e.sqrt(cols[:, 7:8], cols[:, 6:7]), "Sg", Sg)
        pr.wait("vector", Sg, s_sr)
        pr.op("vector", lambda e: e.reciprocal(cols[:, 8:9], cols[:, 7:8]))
        v_ab = pr.op("vector", lambda e: e.scalar_tensor_tensor(
            cols[:, 9:10], cols[:, 2:3], -1.0, cols[:, 8:9], OP.mult, OP.mult), "V", V)
        if _stop(pr, sems, "li3"):
            return True
        pr.wait("scalar", V, v_ab)
        pr.wait("scalar", T, st.get("a1_T", 0))
        if quant:
            s_xn = pr.op("scalar", lambda e, b=xb: e.activation(
                P[3][:], b[:], ACT.Identity, bias=cols[:, 9:10], scale=cols[:, 8:9]), "Sg", Sg)
            pr.wait("vector", Sg, s_xn)
            v_xnb = pr.op("vector", lambda e: e.tensor_copy(xnb[:], P[3][:]), "V", V)
            pr.wait("tensor", V, v_xnb)
        else:
            s_xn = pr.op("scalar", lambda e, b=xb: e.activation(
                xnb[:], b[:], ACT.Identity, bias=cols[:, 9:10], scale=cols[:, 8:9]), "Sg", Sg)
            pr.wait("tensor", Sg, s_xn)
        if _stop(pr, sems, "li4"):
            return True
        ev = {}
        for kk in range(KC):
            bnk = psBF if kk % 2 == 0 else psG
            pr.wait("tensor", V, ev.get(kk - 2, 0))
            dst = xn_feat[:, kk * CH + i * 128: kk * CH + (i + 1) * 128]
            tv = pr.op("tensor", lambda e, b=bnk, k=kk: e.matmul(
                b[:, 0:128], xnb[:, k * 128:(k + 1) * 128], idb[:],
                start=True, stop=True), "T", T)
            pr.wait("vector", T, tv)
            ev[kk] = pr.op("vector", lambda e, b=bnk, d=dst: e.tensor_copy(d, b[:, 0:128]), "V", V)
            if _stop(pr, sems, f"lk{kk}"):
                return True
        st["a1_T"] = pr.cur("T")
        st["a1_ev"] = ev[KC - 1]
        if quant:
            pr.op("vector", lambda e: e.reduce_max(
                cols[:, 10:11], P[3][:], axis=AX, apply_absolute_value=True))
            pr.op("vector", lambda e, ii=i: e.tensor_scalar(
                clipc[:, ii:ii + 1], cols[:, 10:11], 1e-5, None, OP.max))
            pr.op("vector", lambda e, ii=i: e.reciprocal(cols[:, 12:13], clipc[:, ii:ii + 1]))
            pr.op("vector", lambda e: e.tensor_scalar(
                cols[:, 13:14], cols[:, 12:13], 127.0, None, OP.mult))
            pr.op("vector", lambda e, b=xb: e.tensor_scalar(
                b[:], P[3][:], cols[:, 13:14], None, OP.mult))
            pr.op("vector", lambda e, b=xb: e.tensor_scalar(
                P[3][:], b[:], 12582912.0, 12582912.0, OP.add, OP.subtract))
            pr.op("vector", lambda e: e.tensor_copy(i8[:], P[3][:]))
            v_q = pr.op("vector", lambda e: e.tensor_copy(xnb[:], i8[:]), "V", V)
            st["p3_v"] = v_q
            pr.wait("tensor", V, v_q)
            evq = {}
            for kk in range(KC):
                bnk = psBF if kk % 2 == 0 else psG
                pr.wait("tensor", V, evq.get(kk - 2, 0))
                dst = qf[:, kk * CH + i * 128: kk * CH + (i + 1) * 128]
                tv = pr.op("tensor", lambda e, b=bnk, k=kk: e.matmul(
                    b[:, 0:128], xnb[:, k * 128:(k + 1) * 128], idb[:],
                    start=True, stop=True), "T", T)
                pr.wait("vector", T, tv)
                evq[kk] = pr.op("vector", lambda e, b=bnk, d=dst: e.tensor_copy(d, b[:, 0:128]), "V", V)
            st["a1_T"] = pr.cur("T")
            st["a1_ev"] = evq[KC - 1]

    def a2_chunk(stage, c):
        if DBG_A2 == "ld":
            nstage, nchunk = (stage, c + 1) if c + 1 < KC else (stage + 1, 0)
            if nstage < NSTG:
                load_strip(dram["wf"], nchunk, 0, 0, 0)
                st[("fcld", nstage, nchunk)] = load_strip(dram["wc"], nchunk, 2, 0, 2)
            return
        pr.wait("tensor", Sg, st[("fcld", stage, c)])
        pr.wait("tensor", Sg, st.get("fc_ev", 0))
        if c == 0:
            pr.wait("tensor", V, st.get("a1_ev", 0))
        t_f = mm_block((psA, psB),
                       lambda k: strips[k // 8][:, (k % 8) * 128:(k % 8 + 1) * 128],
                       lambda k, h: xn_feat[:, k * CH + h * 512: k * CH + h * 512 + 512],
                       sem="T", semh=T)
        t_c = mm_block((psC, psD),
                       lambda k: strips[2 + k // 8][:, (k % 8) * 128:(k % 8 + 1) * 128],
                       lambda k, h: xn_feat[:, k * CH + h * 512: k * CH + h * 512 + 512],
                       sem="T", semh=T)
        st["last_fmm"], st["last_cmm"] = t_f, t_c
        nstage, nchunk = (stage, c + 1) if c + 1 < KC else (stage + 1, 0)
        if nstage < NSTG:
            load_strip(dram["wf"], nchunk, 0, t_f, 0)
            st[("fcld", nstage, nchunk)] = load_strip(dram["wc"], nchunk, 2, t_c, 2)
        if DBG_A2 == "mm":
            return
        pr.wait("scalar", T, t_c)
        pr.wait("scalar", ld, st.get(("fcst", stage, c - 1),
                                     st.get(("fcst", stage - 1, KC - 1), 0)))
        pr.op("scalar", lambda e: e.activation(sqb[:, 0:512], psA[:, 0:512], ACT.Sigmoid))
        pr.op("scalar", lambda e: e.activation(sqb[:, 512:1024], psB[:, 0:512], ACT.Sigmoid))
        pr.op("scalar", lambda e: e.activation(hstg[:, 0:512], psC[:, 0:512], ACT.Silu))
        s_ev = pr.op("scalar", lambda e: e.activation(hstg[:, 512:1024], psD[:, 0:512], ACT.Silu), "Sg", Sg)
        st["fc_ev"] = s_ev
        if DBG_A2 == "ev":
            return
        pr.wait("sync", Sg, s_ev)
        pr.dma("sync", fc_scr[c * 128:(c + 1) * 128, stage * CH:(stage + 1) * CH], sqb[:], ld, "ld")
        st[("fcst", stage, c)] = pr.dma(
            "sync", fc_scr[D + c * 128: D + (c + 1) * 128, stage * CH:(stage + 1) * CH], hstg[:], ld, "ld")

    load_strip(dram["wf"], 0, 0, 0, 0)
    st[("fcld", 0, 0)] = load_strip(dram["wc"], 0, 2, 0, 2)
    for stage in range(NSTG):
        if DBG_BARRIER and stage > 0:
            for eng in ENGS:
                for smn, smh in (("T", T), ("V", V), ("Sg", Sg), ("ld", ld),
                                 ("gld", gld), ("sld", sld)):
                    pr.wait(eng, smh, pr.cur(smn))
        for i in range(8):
            pr.wait("sync", Sg, st.get(("xcons", stage, i - 2),
                                       st.get(("xcons", stage - 1, 6 + i), 0)))
            st[("xld", i)] = pr.dma(
                "sync", P[i % 2][:], xp[stage * CH + i * 128: stage * CH + (i + 1) * 128, :], ld, "ld")
            if stage < 3:
                pump_prepass(16)
            if stage == 0 and i == 0 and _stop(pr, sems, "x0"):
                return
            if ln_tile(xp, i, "xld", False):
                return
            st[("xcons", stage, i)] = pr.cur("Sg")
            if stage == 0 and _stop(pr, sems, f"lt{i}"):
                return
            if _stop(pr, sems, f"s{stage}lt{i}"):
                return
        if DBG_SCRATCH:
            pr.wait("sync", V, st.get("a1_ev", 0))
            pr.dma("sync", dram["xn_dbg"][stage * 128:(stage + 1) * 128, :],
                   xn_feat[:], ld, "ld")
        for c in range(KC) if not DBG_SKIPA2 else []:
            a2_chunk(stage, c)
            if stage == 0 and _stop(pr, sems, f"a2c{c}"):
                return
            if _stop(pr, sems, f"s{stage}a2c{c}"):
                return
        if _stop(pr, sems, f"A{stage}"):
            return
    pump_prepass(n_pre)
    for mi_ in range(3):
        finalize_matrix(mi_)
    if _stop(pr, sems, "A"):
        return

    fraw = big[:, 0:S]
    craw = big[:, S:2 * S]

    def slc(buf, off, stp, cnt):
        r = buf[:].rearrange("p (j s) -> p j s", s=stp)
        return r[:, 0:cnt, off:off + 1]

    def useg(cnt):
        return U[:].rearrange("p (j s) -> p j s", s=1)[:, 0:cnt, :]

    LOGS = 12
    for c in range(KC):
        pr.wait("sync", V, st.get(("tconv", c - 1), 0))
        tok_f = pr.dma("sync", fraw, fc_scr[c * 128:(c + 1) * 128, :], ld, "ld")
        tok_c = pr.dma("sync", craw, fc_scr[D + c * 128: D + (c + 1) * 128, :], ld, "ld")
        pr.wait("vector", ld, tok_c)
        pr.op("vector", lambda e: e.tensor_copy(t["TRF"][:], fraw))
        v_conv = pr.op("vector", lambda e: e.tensor_copy(t["TRC"][:], craw), "V", V)
        st[("tconv", c)] = v_conv
        for lv in range(1, LOGS + 1):
            stp, half = 1 << lv, 1 << (lv - 1)
            cnt = S // stp
            for s0 in range(0, cnt, CH):
                seg = min(CH, cnt - s0)
                fa = slc(t["TRF"], half - 1, stp, cnt)[:, s0:s0 + seg, :]
                fb = slc(t["TRF"], stp - 1, stp, cnt)[:, s0:s0 + seg, :]
                ca = slc(t["TRC"], half - 1, stp, cnt)[:, s0:s0 + seg, :]
                cb = slc(t["TRC"], stp - 1, stp, cnt)[:, s0:s0 + seg, :]
                us = useg(seg)
                pr.op("vector", lambda e, a=ca, bb=cb, u=us: e.tensor_tensor(u, a, bb, OP.subtract))
                pr.op("vector", lambda e, a=fb, u=us: e.tensor_tensor(u, a, u, OP.mult))
                pr.op("vector", lambda e, bb=cb, u=us: e.tensor_tensor(bb, u, bb, OP.add))
                pr.op("vector", lambda e, a=fa, bb=fb: e.tensor_tensor(bb, a, bb, OP.mult))
        for lv in range(LOGS, 0, -1):
            stp, half = 1 << lv, 1 << (lv - 1)
            cnt2 = S // stp - 1
            if cnt2 <= 0:
                continue
            for s0 in range(0, cnt2, CH):
                seg = min(CH, cnt2 - s0)
                fs = slc(t["TRF"], stp - 1, stp, cnt2 + 1)[:, s0:s0 + seg, :]
                cs = slc(t["TRC"], stp - 1, stp, cnt2 + 1)[:, s0:s0 + seg, :]
                fa = slc(t["TRF"], half - 1, stp, cnt2 + 1)[:, s0 + 1:s0 + 1 + seg, :]
                ca = slc(t["TRC"], half - 1, stp, cnt2 + 1)[:, s0 + 1:s0 + 1 + seg, :]
                us = useg(seg)
                pr.op("vector", lambda e, a=cs, bb=ca, u=us: e.tensor_tensor(u, a, bb, OP.subtract))
                pr.op("vector", lambda e, a=fa, u=us: e.tensor_tensor(u, a, u, OP.mult))
                pr.op("vector", lambda e, bb=ca, u=us: e.tensor_tensor(bb, u, bb, OP.add))
                if lv > 1:
                    pr.op("vector", lambda e, a=fs, bb=fa: e.tensor_tensor(bb, a, bb, OP.mult))
        pr.op("vector", lambda e: e.tensor_scalar(
            U[:], t["TRC"][:, 0:CH], t["sel"][:, 0:1], None, OP.mult))
        for s_ in range(1, NSTG):
            sl_ = t["TRC"][:, s_ * CH:(s_ + 1) * CH]
            pr.op("vector", lambda e, a=sl_, s2=t["sel"][:, s_:s_ + 1]: e.scalar_tensor_tensor(
                U[:], a, s2, U[:], OP.mult, OP.add))
        pr.wait("vector", ld, st.get(("hst", c - 1), 0))
        v_h = pr.op("vector", lambda e: e.tensor_copy(hstg[:], U[:]), "V", V)
        pr.wait("sync", V, v_h)
        st[("hst", c)] = pr.dma("sync", h_scr[c * 128:(c + 1) * 128, :], hstg[:], ld, "ld")
        if _stop(pr, sems, f"T{c}"):
            return
    if _stop(pr, sems, "T"):
        return

    st["a1_T"] = pr.cur("T")
    st["p3_v"] = pr.cur("V")
    for i in range(8):
        pr.wait("sync", Sg, st.get(("ocons", i - 2), 0))
        pr.wait("sync", V, st.get("p3_v", 0))
        st[("old", i)] = pr.dma(
            "sync", P[i % 2][:], dram["xo"][i * 128:(i + 1) * 128, :], ld, "ld")
        if ln_tile(dram["xo"], i, "old", True):
            return
        st[("ocons", i)] = pr.cur("Sg")
    if _stop(pr, sems, "R0"):
        return

    htok = {0: pr.dma("sync", hrd[0][:], h_scr[0:128, :], ld, "ld"),
            1: pr.dma("sync", hrd[1][:], h_scr[128:256, :], ld, "ld")}
    for m in range(KC):
        pr.wait("scalar", ld, htok[m])
        pr.wait("scalar", T, st.get(("hhmm", m - 1), 0))
        hb = hrd[m % 2]
        s_sq = pr.op("scalar", lambda e, b=hb: e.activation(sqb[:], b[:], ACT.Square), "Sg", Sg)
        pr.wait("tensor", Sg, s_sq)
        pr.op("tensor", lambda e, mm=m: e.matmul(
            psE[0:1, 0:512], onesb[:, 0:1], sqb[:, 0:512], start=(mm == 0), stop=(mm == KC - 1)))
        st[("hhmm", m)] = pr.op("tensor", lambda e, mm=m: e.matmul(
            psF[0:1, 0:512], onesb[:, 0:1], sqb[:, 512:1024], start=(mm == 0), stop=(mm == KC - 1)), "T", T)
        if m + 2 < KC:
            pr.wait("sync", Sg, s_sq)
            htok[m + 2] = pr.dma("sync", hrd[m % 2][:], h_scr[(m + 2) * 128:(m + 3) * 128, :], ld, "ld")
    pr.wait("vector", T, st[("hhmm", KC - 1)])
    pr.op("vector", lambda e: e.tensor_scalar(
        L[1][:, 0:512], psE[0:1, 0:512], 1.0 / D, 1e-5, OP.mult, OP.add))
    v_hh = pr.op("vector", lambda e: e.tensor_scalar(
        L[1][:, 512:1024], psF[0:1, 0:512], 1.0 / D, 1e-5, OP.mult, OP.add), "V", V)
    pr.wait("scalar", V, v_hh)
    s_hh = pr.op("scalar", lambda e: e.sqrt(L[2], L[1]), "Sg", Sg)
    pr.wait("vector", Sg, s_hh)
    v_rr = pr.op("vector", lambda e: e.reciprocal(L[3], L[2]), "V", V)
    pr.wait("gpsimd", V, v_rr)
    g_bc = pr.op("gpsimd", lambda e: e.partition_broadcast(T1[:], L[3], channels=128), "G", G)
    if _stop(pr, sems, "R1"):
        return

    wd_st = {"i": 0}

    def pump_wd(n):
        for _ in range(n):
            i = wd_st["i"]
            if i >= 64:
                return
            o2, kq = i // 4, i % 4
            wb = W[i % 2]
            pr.wait("sync", V, wd_st.get(("v", i - 2), 0))
            tk1 = pr.dma("sync", wb[:], dram["wd"][o2 * 128:(o2 + 1) * 128, kq * D: kq * D + CH], ld, "ld")
            pr.wait("vector", ld, tk1)
            pr.wait("vector", sld, wd_st.get(("t", i - 1), 0))
            pr.op("vector", lambda e, b=wb: e.tensor_scalar(
                U[:, 0:CH], b[:], mcols[:, 5:6], -1.0, OP.is_ge, OP.add))
            v1 = pr.op("vector", lambda e, b=wb: e.scalar_tensor_tensor(
                tnat[:, 0:CH], b[:], mcols[:, 4:5], U[:, 0:CH], OP.is_gt, OP.add), "V", V)
            pr.wait("sync", V, v1)
            tk2 = pr.dma("sync", wb[:], dram["wd"][o2 * 128:(o2 + 1) * 128, kq * D + CH:(kq + 1) * D], ld, "ld")
            pr.wait("vector", ld, tk2)
            pr.op("vector", lambda e, b=wb: e.tensor_scalar(
                U[:, 0:CH], b[:], mcols[:, 5:6], -1.0, OP.is_ge, OP.add))
            v2 = pr.op("vector", lambda e, b=wb: e.scalar_tensor_tensor(
                tnat[:, CH:D], b[:], mcols[:, 4:5], U[:, 0:CH], OP.is_gt, OP.add), "V", V)
            wd_st[("v", i)] = v2
            pr.wait("scalar", V, v2)
            pr.wait("scalar", ld, wd_st.get(("s", i - 1), 0))
            in3 = tnat[:].rearrange("p (c h) -> p c h", c=KC)
            out3 = tdT[:].rearrange("p (c h) -> p c h", c=KC)
            tk3 = pr.dmat("scalar", out3, in3, sld, "sld")
            wd_st[("t", i)] = tk3
            pr.wait("sync", sld, tk3)
            wd_st[("s", i)] = pr.dma("sync", td_scr[i * 128:(i + 1) * 128, :], tdT[:], ld, "ld")
            wd_st["i"] = i + 1

    rtok = {0: load_strip(dram["wr"], 0, 0, st["last_cmm"], 2)}
    pr.wait("sync", Sg, pr.cur("Sg"))
    h2tok = {0: pr.dma("sync", hrd[0][:], h_scr[0:128, :], ld, "ld")}
    for m in range(KC):
        pr.wait("tensor", Sg, rtok[m])
        pr.wait("tensor", Sg, st.get("r_ev", 0))
        if m == 0:
            pr.wait("tensor", V, pr.cur("V"))
        t_r = mm_block((psA, psB),
                       lambda k, mm=m: strips[(mm % 2) * 2 + k // 8][:, (k % 8) * 128:(k % 8 + 1) * 128],
                       lambda k, h: xn_feat[:, k * CH + h * 512: k * CH + h * 512 + 512],
                       sem="T", semh=T)
        st[("rmm", m)] = t_r
        if m + 1 < KC:
            rtok[m + 1] = load_strip(dram["wr"], m + 1, ((m + 1) % 2) * 2,
                                     st.get(("rmm", m - 1), 0), 2)
        pump_wd(3)
        pr.wait("scalar", T, t_r)
        pr.wait("scalar", V, st.get(("rhv", m - 1), 0))
        pr.op("scalar", lambda e: e.activation(Q[0][:, 0:512], psA[:, 0:512], ACT.Silu))
        s_r = pr.op("scalar", lambda e: e.activation(Q[0][:, 512:1024], psB[:, 0:512], ACT.Silu), "Sg", Sg)
        st["r_ev"] = s_r
        pr.wait("vector", ld, h2tok[m])
        pr.wait("vector", Sg, s_r)
        if m == 0:
            pr.wait("vector", G, g_bc)
        pr.op("vector", lambda e, b=hrd[m % 2], mm=m: e.tensor_scalar(
            Q[1][:], b[:], gam[:, mm:mm + 1], None, OP.mult))
        pr.op("vector", lambda e: e.tensor_tensor(Q[1][:], Q[1][:], T1[:], OP.mult))
        dst = rh[:, m * CH:(m + 1) * CH]
        v_rh = pr.op("vector", lambda e, d=dst: e.tensor_tensor(d, Q[1][:], Q[0][:], OP.mult), "V", V)
        st[("rhv", m)] = v_rh
        if m + 1 < KC:
            pr.wait("sync", V, st.get(("rhv", m - 1), 0))
            h2tok[m + 1] = pr.dma("sync", hrd[(m + 1) % 2][:], h_scr[(m + 1) * 128:(m + 2) * 128, :], ld, "ld")
    if _stop(pr, sems, "R2"):
        return

    otok = {0: load_strip(dram["wo"], 0, 0, st[("rmm", KC - 1)], 2)}
    pr.wait("tensor", V, pr.cur("V"))
    for m in range(KC):
        pr.wait("tensor", Sg, otok[m])
        pr.wait("tensor", Sg, st.get("y_ev", 0))
        t_y = mm_block((psC, psD),
                       lambda k, mm=m: strips[(mm % 2) * 2 + k // 8][:, (k % 8) * 128:(k % 8 + 1) * 128],
                       lambda k, h: rh[:, k * CH + h * 512: k * CH + h * 512 + 512],
                       sem="T", semh=T)
        st[("ymm", m)] = t_y
        if m + 1 < KC:
            otok[m + 1] = load_strip(dram["wo"], m + 1, ((m + 1) % 2) * 2,
                                     st.get(("ymm", m - 1), 0), 2)
        pump_wd(1)
        pr.wait("scalar", T, t_y)
        pr.wait("scalar", ld, st.get(("sst", m - 1), 0))
        pr.op("scalar", lambda e: e.copy(sqb[:, 0:512], psC[:, 0:512]))
        s_y = pr.op("scalar", lambda e: e.copy(sqb[:, 512:1024], psD[:, 0:512]), "Sg", Sg)
        st["y_ev"] = s_y
        pr.wait("sync", Sg, s_y)
        st[("sst", m)] = pr.dma("sync", s_scr[m * 128:(m + 1) * 128, :], sqb[:, 0:CH], ld, "ld")
    pump_wd(64)
    if _stop(pr, sems, "R3"):
        return

    pr.op("vector", lambda e: e.tensor_copy(clipb[:, 0:8], clipc[:, 0:8]))
    pr.op("vector", lambda e: e.tensor_tensor(cols[:, 16:24], clipc[:, 0:8], clipb[:, 0:8], OP.subtract))
    v_cl = pr.op("vector", lambda e: e.tensor_copy(clipb[:, 8:16], cols[:, 16:24]), "V", V)
    pr.wait("tensor", V, v_cl)
    pr.wait("tensor", Sg, st["y_ev"])
    clv = {}
    for i in range(8):
        bnk = psE if i % 2 == 0 else psF
        pr.wait("tensor", V, clv.get(i - 2, 0))
        pr.op("tensor", lambda e, bk=bnk, ii=i: e.matmul(
            bk[0:1, 0:128], clipb[:, ii:ii + 1], idb[:], start=True, stop=False))
        tpair = pr.op("tensor", lambda e, bk=bnk, ii=i: e.matmul(
            bk[0:1, 0:128], clipb[:, 8 + ii:9 + ii], idb[:], start=False, stop=True), "T", T)
        pr.wait("vector", T, tpair)
        clv[i] = pr.op("vector", lambda e, bk=bnk, ii=i: e.tensor_copy(
            L[1][:, ii * 128:(ii + 1) * 128], bk[0:1, 0:128]), "V", V)
    pr.op("vector", lambda e: e.tensor_tensor(L[2], L[1], L[1], OP.mult))
    pr.op("vector", lambda e: e.tensor_tensor(L[3], L[2], L[1], OP.mult))
    pr.op("vector", lambda e: e.tensor_tensor(mvals[0:1, 3:4], mvals[0:1, 0:1], mvals[0:1, 0:1], OP.mult))
    pr.op("vector", lambda e: e.tensor_tensor(mvals[0:1, 4:5], mvals[0:1, 3:4], mvals[0:1, 1:2], OP.mult))
    pr.op("vector", lambda e: e.tensor_scalar(mvals[0:1, 5:6], mvals[0:1, 4:5], 1.0 / (127.0 ** 3), None, OP.mult))
    pr.op("vector", lambda e: e.tensor_scalar(L[0], L[3], mvals[0:1, 5:6], None, OP.mult))
    v_mm0 = pr.op("vector", lambda e: e.memset(Q[4][:], 0.0), "V", V)

    gu = {}

    def load_gu(w_ap, o, key):
        mi = 0 if key == "g" else 1
        dst_tT = tT[mi]
        pr.wait("sync", V, gu.get(("v", key, o - 1), 0))
        tk1 = pr.dma("sync", W[mi][:], w_ap[o * 128:(o + 1) * 128, 0:CH], ld, "ld")
        pr.wait("vector", ld, tk1)
        pr.wait("vector", sld, gu.get("last_tT", 0))
        pr.op("vector", lambda e, b=W[mi], m=mi: e.tensor_scalar(
            U[:, 0:CH], b[:], mcols[:, 2 * m + 1:2 * m + 2], -1.0, OP.is_ge, OP.add))
        v1 = pr.op("vector", lambda e, b=W[mi], m=mi: e.scalar_tensor_tensor(
            tnat[:, 0:CH], b[:], mcols[:, 2 * m:2 * m + 1], U[:, 0:CH], OP.is_gt, OP.add), "V", V)
        pr.wait("sync", V, v1)
        tk2 = pr.dma("sync", W[mi][:], w_ap[o * 128:(o + 1) * 128, CH:D], ld, "ld")
        pr.wait("vector", ld, tk2)
        pr.op("vector", lambda e, b=W[mi], m=mi: e.tensor_scalar(
            U[:, 0:CH], b[:], mcols[:, 2 * m + 1:2 * m + 2], -1.0, OP.is_ge, OP.add))
        v2 = pr.op("vector", lambda e, b=W[mi], m=mi: e.scalar_tensor_tensor(
            tnat[:, CH:D], b[:], mcols[:, 2 * m:2 * m + 1], U[:, 0:CH], OP.is_gt, OP.add), "V", V)
        gu[("v", key, o)] = v2
        pr.wait("scalar", V, v2)
        pr.wait("scalar", T, gu.get(("mm", key, o - 1), 0))
        in3 = tnat[:].rearrange("p (c h) -> p c h", c=KC)
        out3 = dst_tT[:].rearrange("p (c h) -> p c h", c=KC)
        tk = pr.dmat("scalar", out3, in3, sld, "sld")
        gu[("tT", key, o)] = tk
        gu["last_tT"] = tk

    load_gu(dram["wg"], 0, "g")
    load_gu(dram["wu"], 0, "u")
    for o in range(OC):
        pr.wait("tensor", sld, gu[("tT", "g", o)])
        pr.wait("tensor", sld, gu[("tT", "u", o)])
        pr.wait("tensor", Sg, gu.get("ev_s", 0))
        pr.wait("tensor", V, gu.get("ev_v", 0))
        t_g = mm_block((psA, psB),
                       lambda k: tT[0][:, k * 128:(k + 1) * 128],
                       lambda k, h: qf[:, k * CH + h * 512: k * CH + h * 512 + 512],
                       sem="T", semh=T)
        gu[("mm", "g", o)] = t_g
        t_u = mm_block((psC, psD),
                       lambda k: tT[1][:, k * 128:(k + 1) * 128],
                       lambda k, h: qf[:, k * CH + h * 512: k * CH + h * 512 + 512],
                       sem="T", semh=T)
        gu[("mm", "u", o)] = t_u
        if o + 1 < OC:
            load_gu(dram["wg"], o + 1, "g")
            load_gu(dram["wu"], o + 1, "u")
        ab = Q[2 + (o % 2)]
        pr.wait("scalar", T, t_g)
        pr.op("scalar", lambda e: e.activation(Q[0][:, 0:512], psA[:, 0:512], ACT.Relu))
        s_rg = pr.op("scalar", lambda e: e.activation(Q[0][:, 512:1024], psB[:, 0:512], ACT.Relu), "Sg", Sg)
        pr.wait("vector", Sg, s_rg)
        pr.wait("vector", T, t_u)
        pr.wait("vector", ld, gu.get(("ast", o - 2), 0))
        if o == 0:
            pr.wait("vector", V, v_mm0)
        pr.op("vector", lambda e: e.tensor_tensor(Q[1][:, 0:512], Q[0][:, 0:512], psA[:, 0:512], OP.mult))
        pr.op("vector", lambda e: e.tensor_tensor(Q[1][:, 512:1024], Q[0][:, 512:1024], psB[:, 0:512], OP.mult))
        pr.op("vector", lambda e, b=ab: e.tensor_tensor(b[:, 0:512], Q[1][:, 0:512], psC[:, 0:512], OP.mult))
        v_act = pr.op("vector", lambda e, b=ab: e.tensor_tensor(
            b[:, 512:1024], Q[1][:, 512:1024], psD[:, 0:512], OP.mult), "V", V)
        pr.wait("sync", V, v_act)
        gu[("ast", o)] = pr.dma("sync", act_scr[o * 128:(o + 1) * 128, :], ab[:], ld, "ld")
        pr.wait("scalar", V, v_act)
        s_ag = pr.op("scalar", lambda e, b=ab, oo=o: e.activation(
            Q[5][:], b[:], ACT.Abs, scale=gsub[:, oo:oo + 1]), "Sg", Sg)
        pr.wait("vector", Sg, s_ag)
        v_mx = pr.op("vector", lambda e: e.tensor_tensor(Q[4][:], Q[5][:], Q[4][:], OP.max), "V", V)
        gu["ev_v"] = v_mx
        s_sq = pr.op("scalar", lambda e, b=ab: e.activation(sqb[:], b[:], ACT.Square), "Sg", Sg)
        gu["ev_s"] = s_sq
        pr.wait("tensor", Sg, s_sq)
        for half in range(2):
            bank = psE if half == 0 else psF
            rhs = sqb[:, half * 512:(half + 1) * 512]
            fn3 = lambda e, bk=bank, r=rhs, oo=o: e.matmul(
                bk[0:1, 0:512], onesb[:, 0:1], r, start=(oo == 0), stop=(oo == OC - 1))
            if o == OC - 1 and half == 1:
                t_ss = pr.op("tensor", fn3, "T", T)
            else:
                pr.op("tensor", fn3)
    pr.wait("gpsimd", T, t_ss)
    if _stop(pr, sems, "F1"):
        return

    pr.wait("gpsimd", V, pr.cur("V"))
    g_mx = pr.op("gpsimd", lambda e: e.partition_all_reduce(Q[5][:], Q[4][:], 128, ReduceOp.max), "G", G)
    pr.wait("vector", G, g_mx)
    pr.op("vector", lambda e: e.tensor_copy(L[1], Q[5][0:1, :]))
    pr.wait("vector", T, t_ss)
    pr.op("vector", lambda e: e.tensor_copy(L[2][:, 0:512], psE[0:1, 0:512]))
    pr.op("vector", lambda e: e.tensor_copy(L[2][:, 512:1024], psF[0:1, 0:512]))
    pr.op("vector", lambda e: e.tensor_tensor(L[3], L[0], L[0], OP.mult))
    pr.op("vector", lambda e: e.tensor_tensor(L[2], L[3], L[2], OP.mult))
    v_rs = pr.op("vector", lambda e: e.tensor_scalar(
        L[2], L[2], 1.0 / DI, 1e-6, OP.mult, OP.add), "V", V)
    pr.wait("scalar", V, v_rs)
    s_rs = pr.op("scalar", lambda e: e.sqrt(L[3], L[2]), "Sg", Sg)
    pr.wait("vector", Sg, s_rs)
    pr.op("vector", lambda e: e.reciprocal(L[2], L[3]))
    pr.op("vector", lambda e: e.tensor_tensor(L[1], L[1], L[0], OP.mult))
    pr.op("vector", lambda e: e.tensor_tensor(L[1], L[1], L[2], OP.mult))
    pr.op("vector", lambda e: e.tensor_scalar(L[1], L[1], 1e-5, None, OP.max))
    pr.op("vector", lambda e: e.reciprocal(L[3], L[1]))
    pr.op("vector", lambda e: e.tensor_scalar(L[3], L[3], 127.0, None, OP.mult))
    pr.op("vector", lambda e: e.tensor_tensor(L[2], L[2], L[0], OP.mult))
    v_B = pr.op("vector", lambda e: e.tensor_tensor(L[2], L[2], L[3], OP.mult), "V", V)
    pr.wait("gpsimd", V, v_B)
    g_b1 = pr.op("gpsimd", lambda e: e.partition_broadcast(T1[:], L[2], channels=128), "G", G)
    pr.op("vector", lambda e: e.tensor_scalar(L[1], L[1], 1.0 / 127.0, None, OP.mult))
    v_osc = pr.op("vector", lambda e: e.tensor_scalar(
        L[1], L[1], mvals[0:1, 2:3], None, OP.mult), "V", V)
    pr.wait("gpsimd", V, v_osc)
    g_b2 = pr.op("gpsimd", lambda e: e.partition_broadcast(T2[:], L[1], channels=128), "G", G)
    if _stop(pr, sems, "F2"):
        return

    f3 = {}
    last_dmm = 0
    for hf in range(2):
        tok0 = hf * 512
        for o in range(OC):
            ar = Q[0] if o % 2 == 0 else Q[1]
            pr.wait("sync", V, f3.get(("qa", hf, o - 2), 0))
            atok = pr.dma("sync", ar[:, 0:512], act_scr[o * 128:(o + 1) * 128, tok0:tok0 + 512], ld, "ld")
            pr.wait("vector", ld, atok)
            if o == 0:
                pr.wait("vector", G, g_b2)
                if hf == 1:
                    pr.wait("vector", T, last_dmm)
            src_b = T1[:, tok0:tok0 + 512]
            pr.op("vector", lambda e, a=ar, oo=o, sb2=src_b: e.scalar_tensor_tensor(
                U[:, 0:512], a[:, 0:512], gsub[:, oo:oo + 1], sb2, OP.mult, OP.mult))
            pr.op("vector", lambda e: e.tensor_scalar(
                U[:, 512:1024], U[:, 0:512], 12582912.0, 12582912.0, OP.add, OP.subtract))
            pr.op("vector", lambda e: e.tensor_copy(i8[:, 0:512], U[:, 512:1024]))
            dst = big[:, o * 512:(o + 1) * 512]
            f3[("qa", hf, o)] = pr.op("vector", lambda e, d=dst: e.tensor_copy(d, i8[:, 0:512]), "V", V)
        v_qa = pr.cur("V")
        for j in range(4):
            f3[("xl", hf, j)] = pr.dma(
                "sync", P[j][:], dram["xo"][tok0 + j * 128: tok0 + (j + 1) * 128, :], ld, "ld")
        for o2 in range(KC):
            for kq in range(4):
                pr.wait("sync", T, f3.get(("tmm", hf, o2, kq - 2),
                                          f3.get(("tmm", hf, o2 - 1, kq + 2), 0)))
                tdtok = pr.dma("sync", tT[kq % 2][:],
                               td_scr[(o2 * 4 + kq) * 128:(o2 * 4 + kq + 1) * 128, :], ld, "ld")
                pr.wait("tensor", ld, tdtok)
                if kq == 0:
                    pr.wait("tensor", V, v_qa)
                    pr.wait("tensor", V, f3.get(("dv", hf, o2 - 1), 0))
                for k in range(KC):
                    k2 = kq * KC + k
                    lhs = tT[kq % 2][:, k * 128:(k + 1) * 128]
                    rhs = big[:, k2 * 512:(k2 + 1) * 512]
                    fn4 = lambda e, l=lhs, r=rhs, kk2=k2: e.matmul(
                        psA[:, 0:512], l, r, start=(kk2 == 0), stop=(kk2 == OC - 1))
                    if k == KC - 1:
                        f3[("tmm", hf, o2, kq)] = pr.op("tensor", fn4, "T", T)
                    else:
                        pr.op("tensor", fn4)
            t_d = f3[("tmm", hf, o2, 3)]
            last_dmm = t_d
            pr.wait("sync", V, f3.get(("sgv", hf, o2 - 2), 0))
            stok = pr.dma("sync", sbf[o2 % 2][:], s_scr[o2 * 128:(o2 + 1) * 128, tok0:tok0 + 512], ld, "ld")
            pr.wait("vector", T, t_d)
            osl = T2[:, tok0:tok0 + 512]
            v_d = pr.op("vector", lambda e, o_=osl: e.tensor_tensor(
                U[:, 0:512], psA[:, 0:512], o_, OP.mult), "V", V)
            f3[("dv", hf, o2)] = v_d
            pr.wait("scalar", V, v_d)
            s_g = pr.op("scalar", lambda e: e.activation(Q[5][:, 0:512], U[:, 0:512], ACT.Silu), "Sg", Sg)
            pr.wait("vector", Sg, s_g)
            pr.wait("vector", ld, stok)
            v_sg = pr.op("vector", lambda e, sb2=sbf[o2 % 2]: e.tensor_tensor(
                sqb[:, 0:512], Q[5][:, 0:512], sb2[:], OP.mult), "V", V)
            f3[("sgv", hf, o2)] = v_sg
            pr.wait("tensor", V, v_sg)
            pr.wait("tensor", V, f3.get(("addv", hf, o2 - 1), 0))
            tadds = {}
            for j in range(4):
                bnk = psBF if j % 2 == 0 else psG
                pr.wait("tensor", V, tadds.get(j - 2, 0))
                tvj = pr.op("tensor", lambda e, b=bnk, jj=j: e.matmul(
                    b[:, 0:128], sqb[:, jj * 128:(jj + 1) * 128], idb[:],
                    start=True, stop=True), "T", T)
                pr.wait("vector", T, tvj)
                pr.wait("vector", ld, f3[("xl", hf, j)])
                dst = P[j][:, o2 * 128:(o2 + 1) * 128]
                tadds[j] = pr.op("vector", lambda e, b=bnk, d=dst: e.tensor_tensor(
                    d, b[:, 0:128], d, OP.add), "V", V)
            f3[("addv", hf, o2)] = tadds[3]
        for j in range(4):
            pr.wait("sync", V, f3[("addv", hf, KC - 1)])
            pr.dma("sync", out_d[tok0 + j * 128: tok0 + (j + 1) * 128, :], P[j][:], ld, "ld")
    pr.wait("gpsimd", ld, pr.cur("ld"))
    pr.wait("gpsimd", gld, pr.cur("gld"))
    pr.wait("gpsimd", sld, pr.cur("sld"))


def kernel(L0_attn_x, L0_attn_Wr, L0_attn_Wf, L0_attn_Wc, L0_attn_Wo,
           L0_attn_ln_scale, L0_ffn_w_gate, L0_ffn_w_up, L0_ffn_w_down,
           L0_ffn_subln_weight):
    x = np.ascontiguousarray(np.asarray(L0_attn_x, dtype=np.float32))
    Wr = np.ascontiguousarray(np.asarray(L0_attn_Wr, dtype=np.float32))
    Wf = np.ascontiguousarray(np.asarray(L0_attn_Wf, dtype=np.float32))
    Wc = np.ascontiguousarray(np.asarray(L0_attn_Wc, dtype=np.float32))
    Wo = np.ascontiguousarray(np.asarray(L0_attn_Wo, dtype=np.float32))
    gam = np.ascontiguousarray(np.asarray(L0_attn_ln_scale, dtype=np.float32))
    wg = np.ascontiguousarray(np.asarray(L0_ffn_w_gate, dtype=np.float32))
    wu = np.ascontiguousarray(np.asarray(L0_ffn_w_up, dtype=np.float32))
    wd = np.ascontiguousarray(np.asarray(L0_ffn_w_down, dtype=np.float32))
    gs = np.ascontiguousarray(np.asarray(L0_ffn_subln_weight, dtype=np.float32))

    if "nc" not in _CACHED:
        _CACHED["nc"] = build_nc()
    nc = _CACHED["nc"]

    gam_sb = np.ascontiguousarray(gam.reshape(KC, 128).T)
    gsub_sb = np.ascontiguousarray(gs.reshape(OC, 128).T)
    idb = np.eye(128, dtype=ml_dtypes.bfloat16)

    in_maps = []
    for core in range(8):
        b, p = core // 4, core % 4
        sel = np.zeros((128, NSTG), dtype=np.float32)
        sel[:, p] = 1.0
        in_maps.append({
            "xp": np.ascontiguousarray(x[b]),
            "xo": np.ascontiguousarray(x[b, p * CH:(p + 1) * CH]),
            "sel": sel,
            "wr": Wr, "wf": Wf, "wc": Wc, "wo": Wo,
            "wg": wg, "wu": wu, "wd": wd,
            "gam": gam_sb, "gsub": gsub_sb, "idb": idb,
        })
    res = run_bass_kernel_spmd(nc, in_maps, core_ids=list(range(8)))
    out = np.empty((B, S, D), dtype=np.float32)
    for core in range(8):
        b, p = core // 4, core % 4
        out[b, p * CH:(p + 1) * CH] = res.results[core]["out"]
    return out


# version 08
# speedup vs baseline: 1.0022x; 1.0022x over previous
import numpy as np
import ml_dtypes
import concourse.bacc as bacc
import concourse.mybir as mybir
from concourse.bass_isa import ReduceOp
from concourse.bass_utils import run_bass_kernel_spmd
from contextlib import ExitStack

F32 = mybir.dt.float32
BF16 = mybir.dt.bfloat16
I8 = mybir.dt.int8
OP = mybir.AluOpType
ACT = mybir.ActivationFunctionType
AX = mybir.AxisListType.X

B, S, D, DI = 2, 4096, 2048, 8192
CH = 1024
NSTG = 4
KC = D // 128
OC = DI // 128

ENGS = ("sync", "tensor", "vector", "scalar", "gpsimd")
_CACHED = {}
STOP = "full"
DBG_SCRATCH = False
DBG_BARRIER = False
DBG_SKIPA2 = False
DBG_A2 = ""
DBG_SERDGE = False
EMIT_DRAIN = ("vector",)


class Prog:
    def __init__(self):
        self.ops = {e: [] for e in ENGS}
        self.cnt = {}

    def inc(self, sem, amt):
        self.cnt[sem] = self.cnt.get(sem, 0) + amt
        return self.cnt[sem]

    def cur(self, sem):
        return self.cnt.get(sem, 0)

    def emit(self, eng, fn):
        self.ops[eng].append(fn)

    def wait(self, eng, semh, val):
        if val and val > 0:
            self.emit(eng, lambda e, s=semh, v=val: e.wait_ge(s, v))

    def dma(self, eng, out, in_, semh, semname):
        v = self.inc(semname, 16)
        self.emit(eng, lambda e, o=out, i=in_, s=semh: e.dma_start(out=o, in_=i).then_inc(s, 16))
        return v

    def dmat(self, eng, out, in_, semh, semname):
        v = self.inc(semname, 16)
        self.emit(eng, lambda e, o=out, i=in_, s=semh: e.dma_start(out=o, in_=i, transpose=True).then_inc(s, 16))
        return v

    def op(self, eng, fn, sem=None, semh=None):
        if sem is None:
            self.emit(eng, fn)
            if eng in EMIT_DRAIN:
                self.emit(eng, lambda e: e.drain())
            return None
        v = self.inc(sem, 1)
        self.emit(eng, lambda e, f=fn, s=semh: f(e).then_inc(s, 1))
        if eng in EMIT_DRAIN:
            self.emit(eng, lambda e: e.drain())
        return v


def build_nc():
    nc = bacc.Bacc("TRN2", target_bir_lowering=False, dynamic_dma_scratch_size=8192,
                   detect_race_conditions=False)

    dram = {}
    dram["xp"] = nc.dram_tensor("xp", [S, D], F32, kind="ExternalInput").ap()
    dram["xo"] = nc.dram_tensor("xo", [CH, D], F32, kind="ExternalInput").ap()
    dram["sel"] = nc.dram_tensor("sel", [128, NSTG], F32, kind="ExternalInput").ap()
    for nm, shp in (("wr", [D, D]), ("wf", [D, D]), ("wc", [D, D]), ("wo", [D, D]),
                    ("wg", [DI, D]), ("wu", [DI, D]), ("wd", [D, DI])):
        dram[nm] = nc.dram_tensor(nm, shp, F32, kind="ExternalInput").ap()
    dram["gam"] = nc.dram_tensor("gam", [128, KC], F32, kind="ExternalInput").ap()
    dram["gsub"] = nc.dram_tensor("gsub", [128, OC], F32, kind="ExternalInput").ap()
    dram["idb"] = nc.dram_tensor("idb", [128, 128], BF16, kind="ExternalInput").ap()
    dram["out"] = nc.dram_tensor("out", [CH, D], F32, kind="ExternalOutput").ap()
    scr_kind = "ExternalOutput" if DBG_SCRATCH else "Internal"
    if DBG_SCRATCH:
        dram["xn_dbg"] = nc.dram_tensor("xn_dbg", [NSTG * 128, KC * CH], BF16,
                                        kind="ExternalOutput").ap()
    dram["h_scr"] = nc.dram_tensor("h_scr", [D, CH], BF16, kind=scr_kind).ap()
    dram["fc_scr"] = nc.dram_tensor("fc_scr", [2 * D, S], BF16, kind=scr_kind).ap()
    dram["act_scr"] = nc.dram_tensor("act_scr", [DI, CH], F32, kind=scr_kind).ap()
    dram["s_scr"] = nc.dram_tensor("s_scr", [D, CH], BF16, kind=scr_kind).ap()
    dram["td_scr"] = nc.dram_tensor("td_scr", [OC * 128, D], BF16, kind=scr_kind).ap()

    with ExitStack() as stk:
        def sb(name, shape, dtype=F32):
            return stk.enter_context(nc.sbuf_tensor(name, shape, dtype))

        t = {}
        t["gam"] = sb("gam_s", [128, KC])
        t["gsub"] = sb("gsub_s", [128, OC])
        t["idb"] = sb("idb_s", [128, 128], BF16)
        t["onesb"] = sb("onesb", [128, 1], BF16)
        t["cols"] = sb("cols", [128, 32])
        t["clipc"] = sb("clipc", [128, 8])
        t["clipb"] = sb("clipb", [128, 16], BF16)
        t["sel"] = sb("sel_s", [128, NSTG])
        t["acc3"] = sb("acc3", [128, 4])
        t["mcols"] = sb("mcols", [128, 8])
        t["mvals"] = sb("mvals", [1, 8])
        t["rows"] = sb("rows", [1, 4 * CH])
        trf = sb("TRF", [128, S], F32)
        trc = sb("TRC", [128, S], F32)
        t["TRF"], t["TRC"] = trf, trc
        t["P"] = [trf[:, 0:D], trf[:, D:2 * D], trc[:, 0:D], trc[:, D:2 * D]]
        t["Q"] = [sb(f"Q{i}", [128, CH]) for i in range(6)]
        t["T1"] = sb("T1", [128, CH])
        t["T2"] = sb("T2", [128, CH])
        t["W"] = [sb(f"W{i}", [128, CH]) for i in range(2)]
        t["U"] = sb("U", [128, CH])
        t["big"] = sb("big", [128, 2 * KC * CH], BF16)
        t["qf"] = sb("qf", [128, KC * CH], BF16)
        strip4 = sb("strip4", [128, 4 * CH], BF16)
        t["strips"] = [strip4[:, i * CH:(i + 1) * CH] for i in range(4)]
        t["tT"] = [strip4[:, 0:D], strip4[:, D:2 * D]]
        t["tdT"] = sb("tdT", [128, D], BF16)
        t["xnb"] = sb("xnb", [128, D], BF16)
        t["tnat"] = t["xnb"]
        t["sqb"] = sb("sqb", [128, CH], BF16)
        t["hstg"] = sb("hstg", [128, CH], BF16)
        hrd1 = sb("hrd1", [128, CH], BF16)
        t["hrd"] = [t["hstg"], hrd1]
        t["sbf"] = [hrd1[:, 0:512], hrd1[:, 512:CH]]
        t["i8"] = sb("i8t", [128, D], I8)
        for nm in ("psA", "psB", "psC", "psD", "psE", "psF"):
            t[nm] = stk.enter_context(nc.psum_tensor(nm, [128, 512], F32))
        t["psBF"] = stk.enter_context(nc.psum_tensor("psBF", [128, 512], F32))
        t["psG"] = stk.enter_context(nc.psum_tensor("psG", [128, 512], F32))

        sems = {}
        for name in ("ld", "gld", "sld", "T", "V", "Sg", "G"):
            sems[name] = stk.enter_context(nc.semaphore(name))
        block = stk.enter_context(nc.Block())

        pr = Prog()
        _stop.ctx = (t, dram) if DBG_SCRATCH else None
        plan(pr, sems, t, dram)

        @block.sync
        def _(e):
            for f in pr.ops["sync"]:
                f(e)

        @block.tensor
        def _(e):
            for f in pr.ops["tensor"]:
                f(e)

        @block.vector
        def _(e):
            for f in pr.ops["vector"]:
                f(e)

        @block.scalar
        def _(e):
            for f in pr.ops["scalar"]:
                f(e)

        @block.gpsimd
        def _(e):
            for f in pr.ops["gpsimd"]:
                f(e)

    nc.finalize()
    return nc


def _stop(pr, sems, tag):
    if STOP != tag:
        return False
    ctx = getattr(_stop, "ctx", None)
    if ctx is not None:
        t, dram = ctx
        if "xn_dbg" in dram:
            pr.wait("sync", sems["V"], pr.cur("V"))
            pr.wait("sync", sems["Sg"], pr.cur("Sg"))
            pr.wait("sync", sems["T"], pr.cur("T"))
            pr.dma("sync", dram["xn_dbg"][0:128, :], t["big"][:, 0:KC * CH],
                   sems["ld"], "ld")
    pr.wait("gpsimd", sems["ld"], pr.cur("ld"))
    pr.wait("gpsimd", sems["gld"], pr.cur("gld"))
    pr.wait("gpsimd", sems["sld"], pr.cur("sld"))
    return True


def plan(pr, sems, t, dram):
    ld, gld, sld = sems["ld"], sems["gld"], sems["sld"]
    T, V, Sg, G = sems["T"], sems["V"], sems["Sg"], sems["G"]
    P, Q, W = t["P"], t["Q"], t["W"]
    big, qf, strips, tT = t["big"], t["qf"], t["strips"], t["tT"]
    cols, clipc, clipb = t["cols"], t["clipc"], t["clipb"]
    acc3, mcols, mvals, rows = t["acc3"], t["mcols"], t["mvals"], t["rows"]
    gam, gsub, idb, onesb = t["gam"], t["gsub"], t["idb"], t["onesb"]
    xnb, sqb, hstg, hrd, sbf, i8 = t["xnb"], t["sqb"], t["hstg"], t["hrd"], t["sbf"], t["i8"]
    tnat, U, T1, T2, tdT = t["tnat"], t["U"], t["T1"], t["T2"], t["tdT"]
    psA, psB, psC, psD = t["psA"], t["psB"], t["psC"], t["psD"]
    psE, psF, psBF = t["psE"], t["psF"], t["psBF"]
    psG = t["psG"]
    xp, out_d = dram["xp"], dram["out"]
    fc_scr = dram["fc_scr"]
    h_scr, act_scr, s_scr, td_scr = dram["h_scr"], dram["act_scr"], dram["s_scr"], dram["td_scr"]

    xn_feat = big[:, 0:KC * CH]
    rh = big[:, KC * CH:2 * KC * CH]
    L = [rows[0:1, i * CH:(i + 1) * CH] for i in range(4)]

    st = {}

    ctok = 0
    for dst, src in ((gam[:], dram["gam"]), (gsub[:], dram["gsub"]), (idb[:], dram["idb"]),
                     (t["sel"][:], dram["sel"])):
        ctok = pr.dma("sync", dst, src, ld, "ld")
    pr.wait("vector", ld, ctok)
    pr.op("vector", lambda e: e.memset(acc3[:], 0.0))
    pr.op("vector", lambda e: e.memset(onesb[:], 1.0), "V", V)
    if _stop(pr, sems, "c0"):
        return

    pre_tiles = []
    for mi, wa in enumerate((dram["wg"], dram["wu"], dram["wd"])):
        rb, hb = wa.shape[0] // 128, wa.shape[1] // CH
        for r in range(rb):
            for h in range(hb):
                pre_tiles.append((mi, wa[r * 128:(r + 1) * 128, h * CH:(h + 1) * CH]))
    n_pre = len(pre_tiles)
    pre = {"i": 0}

    def finalize_matrix(mi):
        pr.op("vector", lambda e, m=mi: e.tensor_copy(clipb[:, 0:1], acc3[:, m:m + 1]))
        pr.op("vector", lambda e, m=mi: e.scalar_tensor_tensor(
            cols[:, 30:31], clipb[:, 0:1], -1.0, acc3[:, m:m + 1], OP.mult, OP.add))
        vv = pr.op("vector", lambda e: e.tensor_copy(clipb[:, 1:2], cols[:, 30:31]), "V", V)
        pr.wait("tensor", V, vv)
        pr.op("tensor", lambda e: e.matmul(
            psE[0:1, 0:1], clipb[:, 0:1], onesb[:, 0:1], start=True, stop=False))
        tv = pr.op("tensor", lambda e: e.matmul(
            psE[0:1, 0:1], clipb[:, 1:2], onesb[:, 0:1], start=False, stop=True), "T", T)
        pr.wait("vector", T, tv)
        vv = pr.op("vector", lambda e, m=mi: e.tensor_scalar(
            mvals[0:1, m:m + 1], psE[0:1, 0:1], 1.0 / (DI * D), 1e-5, OP.mult, OP.max), "V", V)
        pr.wait("gpsimd", V, vv)
        gv = pr.op("gpsimd", lambda e, m=mi: e.partition_broadcast(
            cols[:, 31:32], mvals[0:1, m:m + 1], channels=128), "G", G)
        pr.wait("vector", G, gv)
        pr.op("vector", lambda e, m=mi: e.tensor_scalar(
            mcols[:, 2 * m:2 * m + 1], cols[:, 31:32], 0.5, None, OP.mult))
        pr.op("vector", lambda e, m=mi: e.tensor_scalar(
            mcols[:, 2 * m + 1:2 * m + 2], cols[:, 31:32], -0.5, None, OP.mult), "V", V)

    def pump_prepass(n):
        for _ in range(n):
            i = pre["i"]
            if i >= n_pre:
                return
            mi, ap = pre_tiles[i]
            buf = Q[4 + (i % 2)]
            pr.wait("sync", V, pre.get(("v", i - 2), 0))
            tok = pr.dma("sync", buf[:], ap, ld, "ld")
            pr.wait("scalar", ld, tok)
            pr.wait("scalar", V, pre.get(("v", i - 1), 0))
            s_abs = pr.op("scalar", lambda e, b=buf: e.activation(
                U[:, 0:CH], b[:], ACT.Abs, accum_out=cols[:, 29:30]), "Sg", Sg)
            pr.wait("vector", Sg, s_abs)
            pre[("v", i)] = pr.op("vector", lambda e, m=mi: e.tensor_add(
                acc3[:, m:m + 1], acc3[:, m:m + 1], cols[:, 29:30]), "V", V)
            pre["i"] = i + 1

    stg_tok = {}

    def load_strip(w_ap, c, b0, t_guard, sq):
        tok_out = None
        for hk in range(2):
            stg = Q[sq + hk]
            src = w_ap[hk * 1024:(hk + 1) * 1024, c * 128:(c + 1) * 128].rearrange(
                "(k p) m -> p k m", p=128)
            dst3 = stg[:].rearrange("p (k m) -> p k m", k=8)
            pr.wait("sync", Sg, stg_tok.get(sq + hk, 0))
            tk = pr.dma("sync", dst3, src, ld, "ld")
            pr.wait("scalar", ld, tk)
            if hk == 0 and t_guard:
                pr.wait("scalar", T, t_guard)
            tok_out = pr.op("scalar", lambda e, d=strips[b0 + hk], s2=stg: e.copy(d[:], s2[:]), "Sg", Sg)
            stg_tok[sq + hk] = tok_out
        return tok_out

    def mm_block(banks, lhs_fn, rhs_fn, n=512, sem=None, semh=None):
        tok = None
        for half in range(2):
            bank = banks[half]
            for k in range(KC):
                lhs = lhs_fn(k)
                rhs = rhs_fn(k, half)
                fn = lambda e, bk=bank, l=lhs, r=rhs, kk=k, nn=n: e.matmul(
                    bk[:, 0:nn], l, r, start=(kk == 0), stop=(kk == KC - 1))
                if sem is not None and half == 1 and k == KC - 1:
                    tok = pr.op("tensor", fn, sem, semh)
                else:
                    pr.op("tensor", fn)
        return tok

    def ln_tile(x_src, i, tok_key, quant):
        xb = P[i % 2]
        tok = st[(tok_key, i)]
        pr.wait("scalar", ld, tok)
        pr.wait("scalar", V, st.get("p3_v", 0))
        pr.wait("scalar", V, st.get("lnv", 0))
        s_sq = pr.op("scalar", lambda e, b=xb: e.activation(
            P[3][:], b[:], ACT.Square, accum_out=cols[:, 0:1]), "Sg", Sg)
        if _stop(pr, sems, "li1"):
            return True
        pr.wait("vector", ld, tok)
        pr.op("vector", lambda e, b=xb: e.reduce_sum(cols[:, 1:2], b[:], axis=AX))
        pr.wait("vector", Sg, s_sq)
        pr.op("vector", lambda e: e.tensor_scalar(cols[:, 2:3], cols[:, 1:2], 1.0 / D, None, OP.mult))
        pr.op("vector", lambda e: e.tensor_scalar(cols[:, 3:4], cols[:, 0:1], 1.0 / D, None, OP.mult))
        pr.op("vector", lambda e: e.tensor_tensor(cols[:, 4:5], cols[:, 2:3], cols[:, 2:3], OP.mult))
        pr.op("vector", lambda e: e.tensor_tensor(cols[:, 5:6], cols[:, 3:4], cols[:, 4:5], OP.subtract))
        v_ve = pr.op("vector", lambda e: e.tensor_scalar(cols[:, 6:7], cols[:, 5:6], 1e-5, None, OP.add), "V", V)
        st["lnv"] = v_ve
        if _stop(pr, sems, "li2"):
            return True
        pr.wait("scalar", V, v_ve)
        s_sr = pr.op("scalar", lambda e: e.sqrt(cols[:, 7:8], cols[:, 6:7]), "Sg", Sg)
        pr.wait("vector", Sg, s_sr)
        pr.op("vector", lambda e: e.reciprocal(cols[:, 8:9], cols[:, 7:8]))
        v_ab = pr.op("vector", lambda e: e.scalar_tensor_tensor(
            cols[:, 9:10], cols[:, 2:3], -1.0, cols[:, 8:9], OP.mult, OP.mult), "V", V)
        if _stop(pr, sems, "li3"):
            return True
        pr.wait("scalar", V, v_ab)
        pr.wait("scalar", T, st.get("a1_T", 0))
        if quant:
            s_xn = pr.op("scalar", lambda e, b=xb: e.activation(
                P[3][:], b[:], ACT.Identity, bias=cols[:, 9:10], scale=cols[:, 8:9]), "Sg", Sg)
            pr.wait("vector", Sg, s_xn)
            v_xnb = pr.op("vector", lambda e: e.tensor_copy(xnb[:], P[3][:]), "V", V)
            pr.wait("tensor", V, v_xnb)
        else:
            s_xn = pr.op("scalar", lambda e, b=xb: e.activation(
                xnb[:], b[:], ACT.Identity, bias=cols[:, 9:10], scale=cols[:, 8:9]), "Sg", Sg)
            pr.wait("tensor", Sg, s_xn)
        if _stop(pr, sems, "li4"):
            return True
        ev = {}
        for kk in range(KC):
            bnk = psBF if kk % 2 == 0 else psG
            pr.wait("tensor", V, ev.get(kk - 2, 0))
            dst = xn_feat[:, kk * CH + i * 128: kk * CH + (i + 1) * 128]
            tv = pr.op("tensor", lambda e, b=bnk, k=kk: e.matmul(
                b[:, 0:128], xnb[:, k * 128:(k + 1) * 128], idb[:],
                start=True, stop=True), "T", T)
            pr.wait("vector", T, tv)
            ev[kk] = pr.op("vector", lambda e, b=bnk, d=dst: e.tensor_copy(d, b[:, 0:128]), "V", V)
            if _stop(pr, sems, f"lk{kk}"):
                return True
        st["a1_T"] = pr.cur("T")
        st["a1_ev"] = ev[KC - 1]
        if quant:
            pr.op("vector", lambda e: e.reduce_max(
                cols[:, 10:11], P[3][:], axis=AX, apply_absolute_value=True))
            pr.op("vector", lambda e, ii=i: e.tensor_scalar(
                clipc[:, ii:ii + 1], cols[:, 10:11], 1e-5, None, OP.max))
            pr.op("vector", lambda e, ii=i: e.reciprocal(cols[:, 12:13], clipc[:, ii:ii + 1]))
            pr.op("vector", lambda e: e.tensor_scalar(
                cols[:, 13:14], cols[:, 12:13], 127.0, None, OP.mult))
            pr.op("vector", lambda e, b=xb: e.tensor_scalar(
                b[:], P[3][:], cols[:, 13:14], None, OP.mult))
            pr.op("vector", lambda e, b=xb: e.tensor_scalar(
                P[3][:], b[:], 12582912.0, 12582912.0, OP.add, OP.subtract))
            pr.op("vector", lambda e: e.tensor_copy(i8[:], P[3][:]))
            v_q = pr.op("vector", lambda e: e.tensor_copy(xnb[:], i8[:]), "V", V)
            st["p3_v"] = v_q
            pr.wait("tensor", V, v_q)
            evq = {}
            for kk in range(KC):
                bnk = psBF if kk % 2 == 0 else psG
                pr.wait("tensor", V, evq.get(kk - 2, 0))
                dst = qf[:, kk * CH + i * 128: kk * CH + (i + 1) * 128]
                tv = pr.op("tensor", lambda e, b=bnk, k=kk: e.matmul(
                    b[:, 0:128], xnb[:, k * 128:(k + 1) * 128], idb[:],
                    start=True, stop=True), "T", T)
                pr.wait("vector", T, tv)
                evq[kk] = pr.op("vector", lambda e, b=bnk, d=dst: e.tensor_copy(d, b[:, 0:128]), "V", V)
            st["a1_T"] = pr.cur("T")
            st["a1_ev"] = evq[KC - 1]

    def a2_chunk(stage, c):
        if DBG_A2 == "ld":
            nstage, nchunk = (stage, c + 1) if c + 1 < KC else (stage + 1, 0)
            if nstage < NSTG:
                load_strip(dram["wf"], nchunk, 0, 0, 0)
                st[("fcld", nstage, nchunk)] = load_strip(dram["wc"], nchunk, 2, 0, 2)
            return
        pr.wait("tensor", Sg, st[("fcld", stage, c)])
        pr.wait("tensor", Sg, st.get("fc_ev", 0))
        if c == 0:
            pr.wait("tensor", V, st.get("a1_ev", 0))
        t_f = mm_block((psA, psB),
                       lambda k: strips[k // 8][:, (k % 8) * 128:(k % 8 + 1) * 128],
                       lambda k, h: xn_feat[:, k * CH + h * 512: k * CH + h * 512 + 512],
                       sem="T", semh=T)
        t_c = mm_block((psC, psD),
                       lambda k: strips[2 + k // 8][:, (k % 8) * 128:(k % 8 + 1) * 128],
                       lambda k, h: xn_feat[:, k * CH + h * 512: k * CH + h * 512 + 512],
                       sem="T", semh=T)
        st["last_fmm"], st["last_cmm"] = t_f, t_c
        nstage, nchunk = (stage, c + 1) if c + 1 < KC else (stage + 1, 0)
        if nstage < NSTG:
            load_strip(dram["wf"], nchunk, 0, t_f, 0)
            st[("fcld", nstage, nchunk)] = load_strip(dram["wc"], nchunk, 2, t_c, 2)
        if DBG_A2 == "mm":
            return
        pr.wait("scalar", T, t_c)
        pr.wait("scalar", ld, st.get(("fcst", stage, c - 1),
                                     st.get(("fcst", stage - 1, KC - 1), 0)))
        pr.op("scalar", lambda e: e.activation(sqb[:, 0:512], psA[:, 0:512], ACT.Sigmoid))
        pr.op("scalar", lambda e: e.activation(sqb[:, 512:1024], psB[:, 0:512], ACT.Sigmoid))
        pr.op("scalar", lambda e: e.activation(hstg[:, 0:512], psC[:, 0:512], ACT.Silu))
        s_ev = pr.op("scalar", lambda e: e.activation(hstg[:, 512:1024], psD[:, 0:512], ACT.Silu), "Sg", Sg)
        st["fc_ev"] = s_ev
        if DBG_A2 == "ev":
            return
        pr.wait("sync", Sg, s_ev)
        pr.dma("sync", fc_scr[c * 128:(c + 1) * 128, stage * CH:(stage + 1) * CH], sqb[:], ld, "ld")
        st[("fcst", stage, c)] = pr.dma(
            "sync", fc_scr[D + c * 128: D + (c + 1) * 128, stage * CH:(stage + 1) * CH], hstg[:], ld, "ld")

    load_strip(dram["wf"], 0, 0, 0, 0)
    st[("fcld", 0, 0)] = load_strip(dram["wc"], 0, 2, 0, 2)
    for stage in range(NSTG):
        if DBG_BARRIER and stage > 0:
            for eng in ENGS:
                for smn, smh in (("T", T), ("V", V), ("Sg", Sg), ("ld", ld),
                                 ("gld", gld), ("sld", sld)):
                    pr.wait(eng, smh, pr.cur(smn))
        for i in range(8):
            pr.wait("sync", Sg, st.get(("xcons", stage, i - 2),
                                       st.get(("xcons", stage - 1, 6 + i), 0)))
            st[("xld", i)] = pr.dma(
                "sync", P[i % 2][:], xp[stage * CH + i * 128: stage * CH + (i + 1) * 128, :], ld, "ld")
            if stage < 3:
                pump_prepass(16)
            if stage == 0 and i == 0 and _stop(pr, sems, "x0"):
                return
            if ln_tile(xp, i, "xld", False):
                return
            st[("xcons", stage, i)] = pr.cur("Sg")
            if stage == 0 and _stop(pr, sems, f"lt{i}"):
                return
            if _stop(pr, sems, f"s{stage}lt{i}"):
                return
        if DBG_SCRATCH:
            pr.wait("sync", V, st.get("a1_ev", 0))
            pr.dma("sync", dram["xn_dbg"][stage * 128:(stage + 1) * 128, :],
                   xn_feat[:], ld, "ld")
        for c in range(KC) if not DBG_SKIPA2 else []:
            a2_chunk(stage, c)
            if stage == 0 and _stop(pr, sems, f"a2c{c}"):
                return
            if _stop(pr, sems, f"s{stage}a2c{c}"):
                return
        if _stop(pr, sems, f"A{stage}"):
            return
    pump_prepass(n_pre)
    for mi_ in range(3):
        finalize_matrix(mi_)
    if _stop(pr, sems, "A"):
        return

    fraw = big[:, 0:S]
    craw = big[:, S:2 * S]

    def slc(buf, off, stp, cnt):
        r = buf[:].rearrange("p (j s) -> p j s", s=stp)
        return r[:, 0:cnt, off:off + 1]

    def useg(cnt):
        return U[:].rearrange("p (j s) -> p j s", s=1)[:, 0:cnt, :]

    LOGS = 12
    for c in range(KC):
        pr.wait("sync", V, st.get(("tconv", c - 1), 0))
        tok_f = pr.dma("sync", fraw, fc_scr[c * 128:(c + 1) * 128, :], ld, "ld")
        tok_c = pr.dma("sync", craw, fc_scr[D + c * 128: D + (c + 1) * 128, :], ld, "ld")
        pr.wait("vector", ld, tok_c)
        pr.op("vector", lambda e: e.tensor_copy(t["TRF"][:], fraw))
        v_conv = pr.op("vector", lambda e: e.tensor_copy(t["TRC"][:], craw), "V", V)
        st[("tconv", c)] = v_conv
        for lv in range(1, LOGS + 1):
            stp, half = 1 << lv, 1 << (lv - 1)
            cnt = S // stp
            for s0 in range(0, cnt, CH):
                seg = min(CH, cnt - s0)
                fa = slc(t["TRF"], half - 1, stp, cnt)[:, s0:s0 + seg, :]
                fb = slc(t["TRF"], stp - 1, stp, cnt)[:, s0:s0 + seg, :]
                ca = slc(t["TRC"], half - 1, stp, cnt)[:, s0:s0 + seg, :]
                cb = slc(t["TRC"], stp - 1, stp, cnt)[:, s0:s0 + seg, :]
                us = useg(seg)
                pr.op("vector", lambda e, a=ca, bb=cb, u=us: e.tensor_tensor(u, a, bb, OP.subtract))
                pr.op("vector", lambda e, a=fb, u=us: e.tensor_tensor(u, a, u, OP.mult))
                pr.op("vector", lambda e, bb=cb, u=us: e.tensor_tensor(bb, u, bb, OP.add))
                pr.op("vector", lambda e, a=fa, bb=fb: e.tensor_tensor(bb, a, bb, OP.mult))
        for lv in range(LOGS, 0, -1):
            stp, half = 1 << lv, 1 << (lv - 1)
            cnt2 = S // stp - 1
            if cnt2 <= 0:
                continue
            for s0 in range(0, cnt2, CH):
                seg = min(CH, cnt2 - s0)
                fs = slc(t["TRF"], stp - 1, stp, cnt2 + 1)[:, s0:s0 + seg, :]
                cs = slc(t["TRC"], stp - 1, stp, cnt2 + 1)[:, s0:s0 + seg, :]
                fa = slc(t["TRF"], half - 1, stp, cnt2 + 1)[:, s0 + 1:s0 + 1 + seg, :]
                ca = slc(t["TRC"], half - 1, stp, cnt2 + 1)[:, s0 + 1:s0 + 1 + seg, :]
                us = useg(seg)
                pr.op("vector", lambda e, a=cs, bb=ca, u=us: e.tensor_tensor(u, a, bb, OP.subtract))
                pr.op("vector", lambda e, a=fa, u=us: e.tensor_tensor(u, a, u, OP.mult))
                pr.op("vector", lambda e, bb=ca, u=us: e.tensor_tensor(bb, u, bb, OP.add))
                if lv > 1:
                    pr.op("vector", lambda e, a=fs, bb=fa: e.tensor_tensor(bb, a, bb, OP.mult))
        pr.op("vector", lambda e: e.tensor_scalar(
            U[:], t["TRC"][:, 0:CH], t["sel"][:, 0:1], None, OP.mult))
        for s_ in range(1, NSTG):
            sl_ = t["TRC"][:, s_ * CH:(s_ + 1) * CH]
            pr.op("vector", lambda e, a=sl_, s2=t["sel"][:, s_:s_ + 1]: e.scalar_tensor_tensor(
                U[:], a, s2, U[:], OP.mult, OP.add))
        pr.wait("vector", ld, st.get(("hst", c - 1), 0))
        v_h = pr.op("vector", lambda e: e.tensor_copy(hstg[:], U[:]), "V", V)
        pr.wait("sync", V, v_h)
        st[("hst", c)] = pr.dma("sync", h_scr[c * 128:(c + 1) * 128, :], hstg[:], ld, "ld")
        if _stop(pr, sems, f"T{c}"):
            return
    if _stop(pr, sems, "T"):
        return

    st["a1_T"] = pr.cur("T")
    st["p3_v"] = pr.cur("V")
    for i in range(8):
        pr.wait("sync", Sg, st.get(("ocons", i - 2), 0))
        pr.wait("sync", V, st.get("p3_v", 0))
        st[("old", i)] = pr.dma(
            "sync", P[i % 2][:], dram["xo"][i * 128:(i + 1) * 128, :], ld, "ld")
        if ln_tile(dram["xo"], i, "old", True):
            return
        st[("ocons", i)] = pr.cur("Sg")
    if _stop(pr, sems, "R0"):
        return

    htok = {0: pr.dma("sync", hrd[0][:], h_scr[0:128, :], ld, "ld"),
            1: pr.dma("sync", hrd[1][:], h_scr[128:256, :], ld, "ld")}
    for m in range(KC):
        pr.wait("scalar", ld, htok[m])
        pr.wait("scalar", T, st.get(("hhmm", m - 1), 0))
        hb = hrd[m % 2]
        s_sq = pr.op("scalar", lambda e, b=hb: e.activation(sqb[:], b[:], ACT.Square), "Sg", Sg)
        pr.wait("tensor", Sg, s_sq)
        pr.op("tensor", lambda e, mm=m: e.matmul(
            psE[0:1, 0:512], onesb[:, 0:1], sqb[:, 0:512], start=(mm == 0), stop=(mm == KC - 1)))
        st[("hhmm", m)] = pr.op("tensor", lambda e, mm=m: e.matmul(
            psF[0:1, 0:512], onesb[:, 0:1], sqb[:, 512:1024], start=(mm == 0), stop=(mm == KC - 1)), "T", T)
        if m + 2 < KC:
            pr.wait("sync", Sg, s_sq)
            htok[m + 2] = pr.dma("sync", hrd[m % 2][:], h_scr[(m + 2) * 128:(m + 3) * 128, :], ld, "ld")
    pr.wait("vector", T, st[("hhmm", KC - 1)])
    pr.op("vector", lambda e: e.tensor_scalar(
        L[1][:, 0:512], psE[0:1, 0:512], 1.0 / D, 1e-5, OP.mult, OP.add))
    v_hh = pr.op("vector", lambda e: e.tensor_scalar(
        L[1][:, 512:1024], psF[0:1, 0:512], 1.0 / D, 1e-5, OP.mult, OP.add), "V", V)
    pr.wait("scalar", V, v_hh)
    s_hh = pr.op("scalar", lambda e: e.sqrt(L[2], L[1]), "Sg", Sg)
    pr.wait("vector", Sg, s_hh)
    v_rr = pr.op("vector", lambda e: e.reciprocal(L[3], L[2]), "V", V)
    pr.wait("gpsimd", V, v_rr)
    g_bc = pr.op("gpsimd", lambda e: e.partition_broadcast(T1[:], L[3], channels=128), "G", G)
    if _stop(pr, sems, "R1"):
        return

    wd_st = {"i": 0}

    def pump_wd(n):
        for _ in range(n):
            i = wd_st["i"]
            if i >= 64:
                return
            o2, kq = i // 4, i % 4
            wb = W[i % 2]
            pr.wait("sync", V, wd_st.get(("v", i - 2), 0))
            tk1 = pr.dma("sync", wb[:], dram["wd"][o2 * 128:(o2 + 1) * 128, kq * D: kq * D + CH], ld, "ld")
            pr.wait("vector", ld, tk1)
            pr.wait("vector", sld, wd_st.get(("t", i - 1), 0))
            pr.op("vector", lambda e, b=wb: e.tensor_scalar(
                U[:, 0:CH], b[:], mcols[:, 5:6], -1.0, OP.is_ge, OP.add))
            v1 = pr.op("vector", lambda e, b=wb: e.scalar_tensor_tensor(
                tnat[:, 0:CH], b[:], mcols[:, 4:5], U[:, 0:CH], OP.is_gt, OP.add), "V", V)
            pr.wait("sync", V, v1)
            tk2 = pr.dma("sync", wb[:], dram["wd"][o2 * 128:(o2 + 1) * 128, kq * D + CH:(kq + 1) * D], ld, "ld")
            pr.wait("vector", ld, tk2)
            pr.op("vector", lambda e, b=wb: e.tensor_scalar(
                U[:, 0:CH], b[:], mcols[:, 5:6], -1.0, OP.is_ge, OP.add))
            v2 = pr.op("vector", lambda e, b=wb: e.scalar_tensor_tensor(
                tnat[:, CH:D], b[:], mcols[:, 4:5], U[:, 0:CH], OP.is_gt, OP.add), "V", V)
            wd_st[("v", i)] = v2
            pr.wait("scalar", V, v2)
            pr.wait("scalar", ld, wd_st.get(("s", i - 1), 0))
            in3 = tnat[:].rearrange("p (c h) -> p c h", c=KC)
            out3 = tdT[:].rearrange("p (c h) -> p c h", c=KC)
            tk3 = pr.dmat("scalar", out3, in3, sld, "sld")
            wd_st[("t", i)] = tk3
            pr.wait("sync", sld, tk3)
            wd_st[("s", i)] = pr.dma("sync", td_scr[i * 128:(i + 1) * 128, :], tdT[:], ld, "ld")
            wd_st["i"] = i + 1

    rtok = {0: load_strip(dram["wr"], 0, 0, st["last_cmm"], 2)}
    pr.wait("sync", Sg, pr.cur("Sg"))
    h2tok = {0: pr.dma("sync", hrd[0][:], h_scr[0:128, :], ld, "ld")}
    for m in range(KC):
        pr.wait("tensor", Sg, rtok[m])
        pr.wait("tensor", Sg, st.get("r_ev", 0))
        if m == 0:
            pr.wait("tensor", V, pr.cur("V"))
        t_r = mm_block((psA, psB),
                       lambda k, mm=m: strips[(mm % 2) * 2 + k // 8][:, (k % 8) * 128:(k % 8 + 1) * 128],
                       lambda k, h: xn_feat[:, k * CH + h * 512: k * CH + h * 512 + 512],
                       sem="T", semh=T)
        st[("rmm", m)] = t_r
        if m + 1 < KC:
            rtok[m + 1] = load_strip(dram["wr"], m + 1, ((m + 1) % 2) * 2,
                                     st.get(("rmm", m - 1), 0), 2)
        pump_wd(3)
        pr.wait("scalar", T, t_r)
        pr.wait("scalar", V, st.get(("rhv", m - 1), 0))
        pr.op("scalar", lambda e: e.activation(Q[0][:, 0:512], psA[:, 0:512], ACT.Silu))
        s_r = pr.op("scalar", lambda e: e.activation(Q[0][:, 512:1024], psB[:, 0:512], ACT.Silu), "Sg", Sg)
        st["r_ev"] = s_r
        pr.wait("vector", ld, h2tok[m])
        pr.wait("vector", Sg, s_r)
        if m == 0:
            pr.wait("vector", G, g_bc)
        pr.op("vector", lambda e, b=hrd[m % 2], mm=m: e.tensor_scalar(
            Q[1][:], b[:], gam[:, mm:mm + 1], None, OP.mult))
        pr.op("vector", lambda e: e.tensor_tensor(Q[1][:], Q[1][:], T1[:], OP.mult))
        dst = rh[:, m * CH:(m + 1) * CH]
        v_rh = pr.op("vector", lambda e, d=dst: e.tensor_tensor(d, Q[1][:], Q[0][:], OP.mult), "V", V)
        st[("rhv", m)] = v_rh
        if m + 1 < KC:
            pr.wait("sync", V, st.get(("rhv", m - 1), 0))
            h2tok[m + 1] = pr.dma("sync", hrd[(m + 1) % 2][:], h_scr[(m + 1) * 128:(m + 2) * 128, :], ld, "ld")
    if _stop(pr, sems, "R2"):
        return

    otok = {0: load_strip(dram["wo"], 0, 0, st[("rmm", KC - 1)], 2)}
    pr.wait("tensor", V, pr.cur("V"))
    for m in range(KC):
        pr.wait("tensor", Sg, otok[m])
        pr.wait("tensor", Sg, st.get("y_ev", 0))
        t_y = mm_block((psC, psD),
                       lambda k, mm=m: strips[(mm % 2) * 2 + k // 8][:, (k % 8) * 128:(k % 8 + 1) * 128],
                       lambda k, h: rh[:, k * CH + h * 512: k * CH + h * 512 + 512],
                       sem="T", semh=T)
        st[("ymm", m)] = t_y
        if m + 1 < KC:
            otok[m + 1] = load_strip(dram["wo"], m + 1, ((m + 1) % 2) * 2,
                                     st.get(("ymm", m - 1), 0), 2)
        pump_wd(1)
        pr.wait("scalar", T, t_y)
        pr.wait("scalar", ld, st.get(("sst", m - 1), 0))
        pr.op("scalar", lambda e: e.copy(sqb[:, 0:512], psC[:, 0:512]))
        s_y = pr.op("scalar", lambda e: e.copy(sqb[:, 512:1024], psD[:, 0:512]), "Sg", Sg)
        st["y_ev"] = s_y
        pr.wait("sync", Sg, s_y)
        st[("sst", m)] = pr.dma("sync", s_scr[m * 128:(m + 1) * 128, :], sqb[:, 0:CH], ld, "ld")
    pump_wd(64)
    if _stop(pr, sems, "R3"):
        return

    pr.op("vector", lambda e: e.tensor_copy(clipb[:, 0:8], clipc[:, 0:8]))
    pr.op("vector", lambda e: e.tensor_tensor(cols[:, 16:24], clipc[:, 0:8], clipb[:, 0:8], OP.subtract))
    v_cl = pr.op("vector", lambda e: e.tensor_copy(clipb[:, 8:16], cols[:, 16:24]), "V", V)
    pr.wait("tensor", V, v_cl)
    pr.wait("tensor", Sg, st["y_ev"])
    clv = {}
    for i in range(8):
        bnk = psE if i % 2 == 0 else psF
        pr.wait("tensor", V, clv.get(i - 2, 0))
        pr.op("tensor", lambda e, bk=bnk, ii=i: e.matmul(
            bk[0:1, 0:128], clipb[:, ii:ii + 1], idb[:], start=True, stop=False))
        tpair = pr.op("tensor", lambda e, bk=bnk, ii=i: e.matmul(
            bk[0:1, 0:128], clipb[:, 8 + ii:9 + ii], idb[:], start=False, stop=True), "T", T)
        pr.wait("vector", T, tpair)
        clv[i] = pr.op("vector", lambda e, bk=bnk, ii=i: e.tensor_copy(
            L[1][:, ii * 128:(ii + 1) * 128], bk[0:1, 0:128]), "V", V)
    pr.op("vector", lambda e: e.tensor_tensor(L[2], L[1], L[1], OP.mult))
    pr.op("vector", lambda e: e.tensor_tensor(L[3], L[2], L[1], OP.mult))
    pr.op("vector", lambda e: e.tensor_tensor(mvals[0:1, 3:4], mvals[0:1, 0:1], mvals[0:1, 0:1], OP.mult))
    pr.op("vector", lambda e: e.tensor_tensor(mvals[0:1, 4:5], mvals[0:1, 3:4], mvals[0:1, 1:2], OP.mult))
    pr.op("vector", lambda e: e.tensor_scalar(mvals[0:1, 5:6], mvals[0:1, 4:5], 1.0 / (127.0 ** 3), None, OP.mult))
    pr.op("vector", lambda e: e.tensor_scalar(L[0], L[3], mvals[0:1, 5:6], None, OP.mult))
    v_mm0 = pr.op("vector", lambda e: e.memset(Q[4][:], 0.0), "V", V)

    gu = {}

    def load_gu(w_ap, o, key):
        mi = 0 if key == "g" else 1
        dst_tT = tT[mi]
        pr.wait("sync", V, gu.get(("v", key, o - 1), 0))
        tk1 = pr.dma("sync", W[mi][:], w_ap[o * 128:(o + 1) * 128, 0:CH], ld, "ld")
        pr.wait("vector", ld, tk1)
        pr.wait("vector", sld, gu.get("last_tT", 0))
        pr.op("vector", lambda e, b=W[mi], m=mi: e.tensor_scalar(
            U[:, 0:CH], b[:], mcols[:, 2 * m + 1:2 * m + 2], -1.0, OP.is_ge, OP.add))
        v1 = pr.op("vector", lambda e, b=W[mi], m=mi: e.scalar_tensor_tensor(
            tnat[:, 0:CH], b[:], mcols[:, 2 * m:2 * m + 1], U[:, 0:CH], OP.is_gt, OP.add), "V", V)
        pr.wait("sync", V, v1)
        tk2 = pr.dma("sync", W[mi][:], w_ap[o * 128:(o + 1) * 128, CH:D], ld, "ld")
        pr.wait("vector", ld, tk2)
        pr.op("vector", lambda e, b=W[mi], m=mi: e.tensor_scalar(
            U[:, 0:CH], b[:], mcols[:, 2 * m + 1:2 * m + 2], -1.0, OP.is_ge, OP.add))
        v2 = pr.op("vector", lambda e, b=W[mi], m=mi: e.scalar_tensor_tensor(
            tnat[:, CH:D], b[:], mcols[:, 2 * m:2 * m + 1], U[:, 0:CH], OP.is_gt, OP.add), "V", V)
        gu[("v", key, o)] = v2
        pr.wait("scalar", V, v2)
        pr.wait("scalar", T, gu.get(("mm", key, o - 1), 0))
        in3 = tnat[:].rearrange("p (c h) -> p c h", c=KC)
        out3 = dst_tT[:].rearrange("p (c h) -> p c h", c=KC)
        tk = pr.dmat("scalar", out3, in3, sld, "sld")
        gu[("tT", key, o)] = tk
        gu["last_tT"] = tk

    load_gu(dram["wg"], 0, "g")
    load_gu(dram["wu"], 0, "u")
    for o in range(OC):
        pr.wait("tensor", sld, gu[("tT", "g", o)])
        pr.wait("tensor", sld, gu[("tT", "u", o)])
        pr.wait("tensor", Sg, gu.get("ev_s", 0))
        pr.wait("tensor", V, gu.get("ev_v", 0))
        t_g = mm_block((psA, psB),
                       lambda k: tT[0][:, k * 128:(k + 1) * 128],
                       lambda k, h: qf[:, k * CH + h * 512: k * CH + h * 512 + 512],
                       sem="T", semh=T)
        gu[("mm", "g", o)] = t_g
        t_u = mm_block((psC, psD),
                       lambda k: tT[1][:, k * 128:(k + 1) * 128],
                       lambda k, h: qf[:, k * CH + h * 512: k * CH + h * 512 + 512],
                       sem="T", semh=T)
        gu[("mm", "u", o)] = t_u
        if o + 1 < OC:
            load_gu(dram["wg"], o + 1, "g")
            load_gu(dram["wu"], o + 1, "u")
        ab = Q[2 + (o % 2)]
        pr.wait("scalar", T, t_g)
        pr.op("scalar", lambda e: e.activation(Q[0][:, 0:512], psA[:, 0:512], ACT.Relu))
        s_rg = pr.op("scalar", lambda e: e.activation(Q[0][:, 512:1024], psB[:, 0:512], ACT.Relu), "Sg", Sg)
        pr.wait("vector", Sg, s_rg)
        pr.wait("vector", T, t_u)
        pr.wait("vector", ld, gu.get(("ast", o - 2), 0))
        if o == 0:
            pr.wait("vector", V, v_mm0)
        pr.op("vector", lambda e: e.tensor_tensor(Q[1][:, 0:512], Q[0][:, 0:512], psA[:, 0:512], OP.mult))
        pr.op("vector", lambda e: e.tensor_tensor(Q[1][:, 512:1024], Q[0][:, 512:1024], psB[:, 0:512], OP.mult))
        pr.op("vector", lambda e, b=ab: e.tensor_tensor(b[:, 0:512], Q[1][:, 0:512], psC[:, 0:512], OP.mult))
        v_act = pr.op("vector", lambda e, b=ab: e.tensor_tensor(
            b[:, 512:1024], Q[1][:, 512:1024], psD[:, 0:512], OP.mult), "V", V)
        pr.wait("sync", V, v_act)
        gu[("ast", o)] = pr.dma("sync", act_scr[o * 128:(o + 1) * 128, :], ab[:], ld, "ld")
        pr.wait("scalar", V, v_act)
        s_ag = pr.op("scalar", lambda e, b=ab, oo=o: e.activation(
            Q[5][:], b[:], ACT.Abs, scale=gsub[:, oo:oo + 1]), "Sg", Sg)
        pr.wait("vector", Sg, s_ag)
        v_mx = pr.op("vector", lambda e: e.tensor_tensor(Q[4][:], Q[5][:], Q[4][:], OP.max), "V", V)
        gu["ev_v"] = v_mx
        s_sq = pr.op("scalar", lambda e, b=ab: e.activation(sqb[:], b[:], ACT.Square), "Sg", Sg)
        gu["ev_s"] = s_sq
        pr.wait("tensor", Sg, s_sq)
        for half in range(2):
            bank = psE if half == 0 else psF
            rhs = sqb[:, half * 512:(half + 1) * 512]
            fn3 = lambda e, bk=bank, r=rhs, oo=o: e.matmul(
                bk[0:1, 0:512], onesb[:, 0:1], r, start=(oo == 0), stop=(oo == OC - 1))
            if o == OC - 1 and half == 1:
                t_ss = pr.op("tensor", fn3, "T", T)
            else:
                pr.op("tensor", fn3)
    pr.wait("gpsimd", T, t_ss)
    if _stop(pr, sems, "F1"):
        return

    pr.wait("gpsimd", V, pr.cur("V"))
    g_mx = pr.op("gpsimd", lambda e: e.partition_all_reduce(Q[5][:], Q[4][:], 128, ReduceOp.max), "G", G)
    pr.wait("vector", G, g_mx)
    pr.op("vector", lambda e: e.tensor_copy(L[1], Q[5][0:1, :]))
    pr.wait("vector", T, t_ss)
    pr.op("vector", lambda e: e.tensor_copy(L[2][:, 0:512], psE[0:1, 0:512]))
    pr.op("vector", lambda e: e.tensor_copy(L[2][:, 512:1024], psF[0:1, 0:512]))
    pr.op("vector", lambda e: e.tensor_tensor(L[3], L[0], L[0], OP.mult))
    pr.op("vector", lambda e: e.tensor_tensor(L[2], L[3], L[2], OP.mult))
    v_rs = pr.op("vector", lambda e: e.tensor_scalar(
        L[2], L[2], 1.0 / DI, 1e-6, OP.mult, OP.add), "V", V)
    pr.wait("scalar", V, v_rs)
    s_rs = pr.op("scalar", lambda e: e.sqrt(L[3], L[2]), "Sg", Sg)
    pr.wait("vector", Sg, s_rs)
    pr.op("vector", lambda e: e.reciprocal(L[2], L[3]))
    pr.op("vector", lambda e: e.tensor_tensor(L[1], L[1], L[0], OP.mult))
    pr.op("vector", lambda e: e.tensor_tensor(L[1], L[1], L[2], OP.mult))
    pr.op("vector", lambda e: e.tensor_scalar(L[1], L[1], 1e-5, None, OP.max))
    pr.op("vector", lambda e: e.reciprocal(L[3], L[1]))
    pr.op("vector", lambda e: e.tensor_scalar(L[3], L[3], 127.0, None, OP.mult))
    pr.op("vector", lambda e: e.tensor_tensor(L[2], L[2], L[0], OP.mult))
    v_B = pr.op("vector", lambda e: e.tensor_tensor(L[2], L[2], L[3], OP.mult), "V", V)
    pr.wait("gpsimd", V, v_B)
    g_b1 = pr.op("gpsimd", lambda e: e.partition_broadcast(T1[:], L[2], channels=128), "G", G)
    pr.op("vector", lambda e: e.tensor_scalar(L[1], L[1], 1.0 / 127.0, None, OP.mult))
    v_osc = pr.op("vector", lambda e: e.tensor_scalar(
        L[1], L[1], mvals[0:1, 2:3], None, OP.mult), "V", V)
    pr.wait("gpsimd", V, v_osc)
    g_b2 = pr.op("gpsimd", lambda e: e.partition_broadcast(T2[:], L[1], channels=128), "G", G)
    if _stop(pr, sems, "F2"):
        return

    f3 = {}
    last_dmm = 0
    for hf in range(2):
        tok0 = hf * 512
        for o in range(OC):
            ar = Q[0] if o % 2 == 0 else Q[1]
            pr.wait("sync", V, f3.get(("qa", hf, o - 2), 0))
            atok = pr.dma("sync", ar[:, 0:512], act_scr[o * 128:(o + 1) * 128, tok0:tok0 + 512], ld, "ld")
            pr.wait("vector", ld, atok)
            if o == 0:
                pr.wait("vector", G, g_b2)
                if hf == 1:
                    pr.wait("vector", T, last_dmm)
            src_b = T1[:, tok0:tok0 + 512]
            pr.op("vector", lambda e, a=ar, oo=o, sb2=src_b: e.scalar_tensor_tensor(
                U[:, 0:512], a[:, 0:512], gsub[:, oo:oo + 1], sb2, OP.mult, OP.mult))
            pr.op("vector", lambda e: e.tensor_scalar(
                U[:, 512:1024], U[:, 0:512], 12582912.0, 12582912.0, OP.add, OP.subtract))
            pr.op("vector", lambda e: e.tensor_copy(i8[:, 0:512], U[:, 512:1024]))
            dst = big[:, o * 512:(o + 1) * 512]
            f3[("qa", hf, o)] = pr.op("vector", lambda e, d=dst: e.tensor_copy(d, i8[:, 0:512]), "V", V)
        v_qa = pr.cur("V")
        for j in range(4):
            f3[("xl", hf, j)] = pr.dma(
                "sync", P[j][:], dram["xo"][tok0 + j * 128: tok0 + (j + 1) * 128, :], ld, "ld")
        for o2 in range(KC):
            for kq in range(4):
                pr.wait("sync", T, f3.get(("tmm", hf, o2, kq - 2),
                                          f3.get(("tmm", hf, o2 - 1, kq + 2), 0)))
                tdtok = pr.dma("sync", tT[kq % 2][:],
                               td_scr[(o2 * 4 + kq) * 128:(o2 * 4 + kq + 1) * 128, :], ld, "ld")
                pr.wait("tensor", ld, tdtok)
                if kq == 0:
                    pr.wait("tensor", V, v_qa)
                    pr.wait("tensor", V, f3.get(("dv", hf, o2 - 1), 0))
                for k in range(KC):
                    k2 = kq * KC + k
                    lhs = tT[kq % 2][:, k * 128:(k + 1) * 128]
                    rhs = big[:, k2 * 512:(k2 + 1) * 512]
                    fn4 = lambda e, l=lhs, r=rhs, kk2=k2: e.matmul(
                        psA[:, 0:512], l, r, start=(kk2 == 0), stop=(kk2 == OC - 1))
                    if k == KC - 1:
                        f3[("tmm", hf, o2, kq)] = pr.op("tensor", fn4, "T", T)
                    else:
                        pr.op("tensor", fn4)
            t_d = f3[("tmm", hf, o2, 3)]
            last_dmm = t_d
            pr.wait("sync", V, f3.get(("sgv", hf, o2 - 2), 0))
            stok = pr.dma("sync", sbf[o2 % 2][:], s_scr[o2 * 128:(o2 + 1) * 128, tok0:tok0 + 512], ld, "ld")
            pr.wait("vector", T, t_d)
            osl = T2[:, tok0:tok0 + 512]
            v_d = pr.op("vector", lambda e, o_=osl: e.tensor_tensor(
                U[:, 0:512], psA[:, 0:512], o_, OP.mult), "V", V)
            f3[("dv", hf, o2)] = v_d
            pr.wait("scalar", V, v_d)
            s_g = pr.op("scalar", lambda e: e.activation(Q[5][:, 0:512], U[:, 0:512], ACT.Silu), "Sg", Sg)
            pr.wait("vector", Sg, s_g)
            pr.wait("vector", ld, stok)
            v_sg = pr.op("vector", lambda e, sb2=sbf[o2 % 2]: e.tensor_tensor(
                sqb[:, 0:512], Q[5][:, 0:512], sb2[:], OP.mult), "V", V)
            f3[("sgv", hf, o2)] = v_sg
            pr.wait("tensor", V, v_sg)
            pr.wait("tensor", V, f3.get(("addv", hf, o2 - 1), 0))
            tadds = {}
            for j in range(4):
                bnk = psBF if j % 2 == 0 else psG
                pr.wait("tensor", V, tadds.get(j - 2, 0))
                tvj = pr.op("tensor", lambda e, b=bnk, jj=j: e.matmul(
                    b[:, 0:128], sqb[:, jj * 128:(jj + 1) * 128], idb[:],
                    start=True, stop=True), "T", T)
                pr.wait("vector", T, tvj)
                pr.wait("vector", ld, f3[("xl", hf, j)])
                dst = P[j][:, o2 * 128:(o2 + 1) * 128]
                tadds[j] = pr.op("vector", lambda e, b=bnk, d=dst: e.tensor_tensor(
                    d, b[:, 0:128], d, OP.add), "V", V)
            f3[("addv", hf, o2)] = tadds[3]
        for j in range(4):
            pr.wait("sync", V, f3[("addv", hf, KC - 1)])
            pr.dma("sync", out_d[tok0 + j * 128: tok0 + (j + 1) * 128, :], P[j][:], ld, "ld")
    pr.wait("gpsimd", ld, pr.cur("ld"))
    pr.wait("gpsimd", gld, pr.cur("gld"))
    pr.wait("gpsimd", sld, pr.cur("sld"))


def kernel(L0_attn_x, L0_attn_Wr, L0_attn_Wf, L0_attn_Wc, L0_attn_Wo,
           L0_attn_ln_scale, L0_ffn_w_gate, L0_ffn_w_up, L0_ffn_w_down,
           L0_ffn_subln_weight):
    x = np.ascontiguousarray(np.asarray(L0_attn_x, dtype=np.float32))
    Wr = np.ascontiguousarray(np.asarray(L0_attn_Wr, dtype=np.float32))
    Wf = np.ascontiguousarray(np.asarray(L0_attn_Wf, dtype=np.float32))
    Wc = np.ascontiguousarray(np.asarray(L0_attn_Wc, dtype=np.float32))
    Wo = np.ascontiguousarray(np.asarray(L0_attn_Wo, dtype=np.float32))
    gam = np.ascontiguousarray(np.asarray(L0_attn_ln_scale, dtype=np.float32))
    wg = np.ascontiguousarray(np.asarray(L0_ffn_w_gate, dtype=np.float32))
    wu = np.ascontiguousarray(np.asarray(L0_ffn_w_up, dtype=np.float32))
    wd = np.ascontiguousarray(np.asarray(L0_ffn_w_down, dtype=np.float32))
    gs = np.ascontiguousarray(np.asarray(L0_ffn_subln_weight, dtype=np.float32))

    if "nc" not in _CACHED:
        _CACHED["nc"] = build_nc()
    nc = _CACHED["nc"]

    gam_sb = np.ascontiguousarray(gam.reshape(KC, 128).T)
    gsub_sb = np.ascontiguousarray(gs.reshape(OC, 128).T)
    idb = np.eye(128, dtype=ml_dtypes.bfloat16)

    in_maps = []
    for core in range(8):
        b, p = core // 4, core % 4
        sel = np.zeros((128, NSTG), dtype=np.float32)
        sel[:, p] = 1.0
        in_maps.append({
            "xp": np.ascontiguousarray(x[b]),
            "xo": np.ascontiguousarray(x[b, p * CH:(p + 1) * CH]),
            "sel": sel,
            "wr": Wr, "wf": Wf, "wc": Wc, "wo": Wo,
            "wg": wg, "wu": wu, "wd": wd,
            "gam": gam_sb, "gsub": gsub_sb, "idb": idb,
        })
    res = run_bass_kernel_spmd(nc, in_maps, core_ids=list(range(8)))
    out = np.empty((B, S, D), dtype=np.float32)
    for core in range(8):
        b, p = core // 4, core % 4
        out[b, p * CH:(p + 1) * CH] = res.results[core]["out"]
    return out
